# version 10
# baseline (speedup 1.0000x reference)
from contextlib import ExitStack

import numpy as np
import ml_dtypes
import concourse.bass as bass
import concourse.mybir as mybir
from concourse.bass_utils import run_bass_kernel_spmd

F32 = mybir.dt.float32
BF16 = mybir.dt.bfloat16
U32 = mybir.dt.uint32
AF = mybir.ActivationFunctionType
ALU = mybir.AluOpType
AX = mybir.AxisListType

NCORES = 8
P = 128
D = 1024
S = 8192
B = 2
H = 16
DH = 64
E3 = 3 * D
HID = 4096
NT = S // P
NCLS = 8
TPC = 2 * NCLS
TCORE = TPC * P
EXP_SHIFT = -25.0
PAIR_POLL = True
SHARED_KV = True
AMAX_CLAMP = 2.0 ** -96
MAGIC = float(1.5 * 2.0 ** 20)
EXPO_MASK = 0x7F800000


class Sem:
    def __init__(self, nc, name):
        self.h = nc.alloc_semaphore(name)
        self.n = 0

    def inc(self, ins, k=1):
        ins.then_inc(self.h, k)
        self.n += k
        return self.n


class AutoChain:

    _PASS = {"wait_ge", "engine", "bass"}

    def __init__(self, eng, sem):
        self._eng = eng
        self._sem = sem

    def __getattr__(self, name):
        f = getattr(self._eng, name)
        if name in AutoChain._PASS or not callable(f):
            return f
        sem = self._sem
        eng = self._eng

        def wrap(*a, **k):
            if sem.n:
                eng.wait_ge(sem.h, sem.n)
            ins = f(*a, **k)
            sem.inc(ins)
            return ins

        return wrap


def build():
    nc = bass.Bass("TRN2", target_bir_lowering=False)

    x_kv = nc.dram_tensor("x_kv", [S, D], F32, kind="ExternalInput").ap()
    x_q = nc.dram_tensor("x_q", [TCORE, D], F32, kind="ExternalInput").ap()
    w_qkv = nc.dram_tensor("w_qkv", [D, E3], F32, kind="ExternalInput").ap()
    w_fc = nc.dram_tensor("w_fc", [HID, D], F32, kind="ExternalInput").ap()
    b_fc = nc.dram_tensor("b_fc", [1, HID], F32, kind="ExternalInput").ap()
    w_proj = nc.dram_tensor("w_proj", [D, HID], F32, kind="ExternalInput").ap()
    b_proj = nc.dram_tensor("b_proj", [1, D], F32, kind="ExternalInput").ap()
    masks = nc.dram_tensor("masks", [P, NT * P], BF16, kind="ExternalInput").ap()
    y = nc.dram_tensor("y", [TCORE, D], F32, kind="ExternalOutput").ap()

    _shr = {"addr_space": "Shared"} if SHARED_KV else {}
    kT_sh = nc.dram_tensor("kT_sh", [B * D, S], BF16, kind="Internal",
                           **_shr).ap()
    v_sh = nc.dram_tensor("v_sh", [B * S, H * 65], BF16, kind="Internal",
                          **_shr).ap()
    kT_loc = nc.dram_tensor("kT_loc", [D, S], BF16, kind="Internal").ap()
    v_loc = nc.dram_tensor("v_loc", [S, H * 65], BF16, kind="Internal").ap()
    kvflag_sh = nc.dram_tensor("kvflag_sh", [2, 8], F32, kind="Internal",
                               **_shr).ap()
    wfcqT_d = nc.dram_tensor("wfcqT_d", [D, HID], BF16, kind="Internal").ap()
    wprojqT_d = nc.dram_tensor("wprojqT_d", [HID, D], BF16, kind="Internal").ap()

    ident_d = nc.inline_tensor(np.eye(P, dtype=ml_dtypes.bfloat16), name="idbf").ap()
    ones_d = nc.inline_tensor(np.ones((1, P), np.float32), name="onef").ap()

    s_c = Sem(nc, "s_c")
    s_x = [Sem(nc, "s_x0"), Sem(nc, "s_x1")]
    s_wr = Sem(nc, "s_wr")
    s_br = Sem(nc, "s_br")
    s_wf = [Sem(nc, "s_wf0"), Sem(nc, "s_wf1")]
    s_wp = [Sem(nc, "s_wp0"), Sem(nc, "s_wp1")]
    s_kv = [Sem(nc, "s_kv0"), Sem(nc, "s_kv1")]
    s_y = [Sem(nc, "s_y0"), Sem(nc, "s_y1")]
    s_wq = Sem(nc, "s_wq")
    s_sk = [Sem(nc, "s_sk0"), Sem(nc, "s_sk1")]
    s_sv = [Sem(nc, "s_sv0"), Sem(nc, "s_sv1")]
    s_wt = Sem(nc, "s_wt")
    s_wr2 = Sem(nc, "s_wr2")
    s_wt2 = Sem(nc, "s_wt2")
    s_pr = Sem(nc, "s_pr")
    s_bulk = Sem(nc, "s_bulk")
    s_flg = Sem(nc, "s_flg")

    sy, gp, te = nc.sync, nc.gpsimd, nc.tensor
    s_t = Sem(nc, "s_t")
    s_v = Sem(nc, "s_v")
    s_sc = Sem(nc, "s_sc")
    ve = AutoChain(nc.vector, s_v)
    sc = AutoChain(nc.scalar, s_sc)

    def W(eng, sem, val):
        if val > 0:
            eng.wait_ge(sem.h, val)

    ALL_SEMS = [s_c, s_x[0], s_x[1], s_wr, s_br, s_wf[0], s_wf[1], s_wp[0],
                s_wp[1], s_kv[0], s_kv[1], s_y[0], s_y[1], s_wq, s_sk[0],
                s_sk[1], s_sv[0], s_sv[1], s_wt, s_wr2, s_wt2, s_pr, s_t,
                s_v, s_sc]

    def phase_join():
        for e in (sy, gp, te, nc.vector, nc.scalar):
            for sem in ALL_SEMS:
                W(e, sem, sem.n)

    ident_sb = nc.alloc_sbuf_tensor("ident_sb", [P, P], BF16)
    ones_sb = nc.alloc_sbuf_tensor("ones_sb", [1, P], F32)
    qT_sb = nc.alloc_sbuf_tensor("qT_sb", [P, 8 * TCORE], BF16)
    attn_sb = nc.alloc_sbuf_tensor("attn_sb", [P, TPC * D], BF16)
    ffn_sb = nc.alloc_sbuf_tensor("ffn_sb", [P, TPC * D], BF16)
    mr_sb = nc.alloc_sbuf_tensor("mr_sb", [P, 2 * TPC], F32)
    scr = {}
    flgsb = nc.alloc_sbuf_tensor("flgsb", [1, 8], F32)
    s_am = nc.alloc_sbuf_tensor("scr_am", [P, HID // 32], F32)
    s_mm = nc.alloc_sbuf_tensor("scr_m", [P, HID // 32], F32)
    s_s1 = nc.alloc_sbuf_tensor("scr_s1", [P, 32], F32)
    expb = nc.alloc_sbuf_tensor("expb", [P, 1], F32)

    ve.memset(expb[:], EXP_SHIFT)
    c_expb = s_v.n
    ld = sy.dma_start(ident_sb[:], ident_d)
    s_c.inc(ld, 16)
    ld = sy.dma_start(ones_sb[:], ones_d)
    s_c.inc(ld, 16)
    c_consts = s_c.n

    def quant_free(out_bf, in_f32, width):
        nb = width // 32
        x3 = in_f32.rearrange("p (b k) -> p b k", k=32)
        ve.tensor_reduce(s_am[:, 0:nb], x3, AX.X, ALU.max,
                         apply_absolute_value=True)
        ve.tensor_scalar(s_mm[:, 0:nb].bitcast(U32), s_am[:, 0:nb].bitcast(U32),
                         EXPO_MASK, None, ALU.bitwise_and)
        ve.tensor_scalar_mul(s_mm[:, 0:nb], s_mm[:, 0:nb], MAGIC)
        mb = s_mm[:, 0:nb].unsqueeze(2).broadcast_to((P, nb, 32))
        t3 = scr["a"][:, 0:width].rearrange("p (b k) -> p b k", k=32)
        o3 = out_bf.rearrange("p (b k) -> p b k", k=32)
        ve.tensor_tensor(t3, x3, mb, ALU.add)
        ve.tensor_tensor(o3, t3, mb, ALU.subtract)

    def stats_acc(tile_f32, sem_load, c_load, tt):
        W(sc, sem_load, c_load)
        sc.activation(scr["b"][:, 0:D], tile_f32, AF.Copy,
                      accum_out=s_s1[:, tt:tt + 1])
        sc.activation(scr["b"][:, 0:D], tile_f32, AF.Square,
                      accum_out=s_s1[:, 4 + tt:5 + tt])
        return s_sc.n

    def stats_finalize(nt, slot0_mr):
        ve.tensor_scalar_mul(s_s1[:, 8:8 + nt], s_s1[:, 0:nt], 1.0 / D)
        ve.tensor_scalar_mul(s_s1[:, 16:16 + nt], s_s1[:, 4:4 + nt], 1.0 / D)
        ve.tensor_tensor(s_s1[:, 20:20 + nt], s_s1[:, 8:8 + nt],
                         s_s1[:, 8:8 + nt], ALU.mult)
        ve.tensor_tensor(s_s1[:, 20:20 + nt], s_s1[:, 16:16 + nt],
                         s_s1[:, 20:20 + nt], ALU.subtract)
        ve.tensor_scalar_add(s_s1[:, 20:20 + nt], s_s1[:, 20:20 + nt], 1e-5)
        c_var = s_v.n
        W(sc, s_v, c_var)
        sc.sqrt(s_s1[:, 24:24 + nt], s_s1[:, 20:20 + nt])
        c_sd = s_sc.n
        W(ve, s_sc, c_sd)
        ve.reciprocal(s_s1[:, 12:12 + nt], s_s1[:, 24:24 + nt])
        if slot0_mr is not None:
            ve.tensor_copy(mr_sb[:].rearrange("p (s two) -> p s two", two=2)
                           [:, slot0_mr:slot0_mr + nt, 0:1],
                           s_s1[:, 8:8 + nt].unsqueeze(2))
            ve.tensor_copy(mr_sb[:].rearrange("p (s two) -> p s two", two=2)
                           [:, slot0_mr:slot0_mr + nt, 1:2],
                           s_s1[:, 12:12 + nt].unsqueeze(2))
        return s_v.n

    def stats_norm(tile_f32, sem_load, c_load, slot_mr, xn_out, prev_sb_reader_v):
        W(sc, sem_load, c_load)
        W(sc, s_v, prev_sb_reader_v)
        sc.square(scr["b"][:, 0:D], tile_f32)
        c_sq = s_sc.n
        W(ve, sem_load, c_load)
        ve.reduce_sum(s_s1[:, 0:1], tile_f32, AX.X)
        W(ve, s_sc, c_sq)
        ve.reduce_sum(s_s1[:, 1:2], scr["b"][:, 0:D], AX.X)
        ve.tensor_scalar_mul(s_s1[:, 2:3], s_s1[:, 0:1], 1.0 / D)
        ve.tensor_scalar_mul(s_s1[:, 3:4], s_s1[:, 1:2], 1.0 / D)
        ve.tensor_tensor(s_s1[:, 4:5], s_s1[:, 2:3], s_s1[:, 2:3], ALU.mult)
        ve.tensor_tensor(s_s1[:, 5:6], s_s1[:, 3:4], s_s1[:, 4:5], ALU.subtract)
        ve.tensor_scalar_add(s_s1[:, 5:6], s_s1[:, 5:6], 1e-5)
        c_var = s_v.n
        W(sc, s_v, c_var)
        sc.sqrt(s_s1[:, 6:7], s_s1[:, 5:6])
        c_sd = s_sc.n
        W(ve, s_sc, c_sd)
        ve.reciprocal(s_s1[:, 7:8], s_s1[:, 6:7])
        if slot_mr is not None:
            ve.tensor_copy(mr_sb[:, 2 * slot_mr:2 * slot_mr + 1], s_s1[:, 2:3])
            ve.tensor_copy(mr_sb[:, 2 * slot_mr + 1:2 * slot_mr + 2], s_s1[:, 7:8])
        ve.tensor_scalar(xn_out, tile_f32, s_s1[:, 2:3],
                         s_s1[:, 7:8], ALU.subtract, ALU.mult)
        return s_v.n, c_sq

    with nc.sbuf_tensor("wq_sb", [P, 8 * E3], BF16) as wq_sb:
        for dc in range(8):
            ins = gp.dma_start(wq_sb[:, dc * E3:(dc + 1) * E3],
                               w_qkv[dc * P:(dc + 1) * P, :])
            c_wq = s_wq.inc(ins, 16)

        with ExitStack() as _wst:
            _e = _wst.enter_context
            WMAX = 2048
            wraw = [_e(nc.sbuf_tensor(f"wraw{i}", [P, WMAX], F32))
                    for i in range(2)]
            scr["a"] = _e(nc.sbuf_tensor("scrWa", [P, D], F32))
            wqbf = [_e(nc.sbuf_tensor(f"wqbf{i}", [P, WMAX], BF16))
                    for i in range(2)]
            wtr = [_e(nc.sbuf_tensor(f"wtr{i}", [P, WMAX], BF16))
                   for i in range(2)]
            pw = [_e(nc.psum_tensor(f"pw{i}", [P, P], BF16)) for i in range(2)]
            tr_rel = [0, 0]
            s_wld = [s_wr, s_wr2]
            s_wst = [s_wt, s_wt2]
            c_store = [0, 0]
            c_vq = [0, 0]
            c_tr = [0, 0]
            wg = [0]

            def w_quant_transpose(src, rows, width, dst):
                dst3 = dst.rearrange("(c p) n -> p c n", p=P)
                for ch in range(rows // P):
                    for wo in range(0, width, WMAX):
                        pw_ = min(WMAX, width - wo)
                        p = wg[0] % 2
                        wg[0] += 1
                        W(sy, s_v, c_vq[p])
                        ins = sy.dma_start(
                            wraw[p][:, 0:pw_],
                            src[ch * P:(ch + 1) * P, wo:wo + pw_])
                        c_load = s_wld[p].inc(ins, 16)
                        W(ve, s_wld[p], c_load)
                        W(ve, s_t, c_tr[p])
                        for qo in range(0, pw_, D):
                            qw = min(D, pw_ - qo)
                            quant_free(wqbf[p][:, qo:qo + qw],
                                       wraw[p][:, qo:qo + qw], qw)
                        c_vq[p] = s_v.n
                        W(ve, s_wst[p], c_store[p])
                        for j in range(pw_ // P):
                            W(te, s_v, c_vq[p])
                            W(te, s_v, tr_rel[j % 2])
                            c_tp = s_t.inc(te.transpose(
                                pw[j % 2][:], wqbf[p][:, j * P:(j + 1) * P],
                                ident_sb[:]))
                            W(ve, s_t, c_tp)
                            ve.tensor_copy(wtr[p][:, j * P:(j + 1) * P],
                                           pw[j % 2][:])
                            tr_rel[j % 2] = s_v.n
                        c_tr[p] = s_t.n
                        W(gp, s_v, tr_rel[(pw_ // P - 1) % 2])
                        W(gp, s_v, tr_rel[(pw_ // P) % 2])
                        ins = gp.dma_start(
                            dst3[:, wo // P:(wo + pw_) // P,
                                 ch * P:(ch + 1) * P],
                            wtr[p][:, 0:pw_].rearrange("p (c q) -> p c q", q=P))
                        c_store[p] = s_wst[p].inc(ins, 16)

            te.wait_ge(s_c.h, c_consts)
            w_quant_transpose(w_fc, HID, D, wfcqT_d)
            w_quant_transpose(w_proj, D, HID, wprojqT_d)
        c_wprep = (s_wt.n, s_wt2.n)

        phase_join()
        with ExitStack() as _kst:
            _e = _kst.enter_context
            xf = [_e(nc.sbuf_tensor(f"xf{i}", [P, D], F32)) for i in range(4)]
            scr["b"] = _e(nc.sbuf_tensor("scrKb", [P, D], F32))
            xn = [_e(nc.sbuf_tensor(f"xn{i}", [P, D], BF16)) for i in range(4)]
            xT = [_e(nc.sbuf_tensor(f"xT{i}", [P, 8 * 512], BF16)) for i in range(2)]
            kst = [_e(nc.sbuf_tensor(f"kst{i}", [P, 512], BF16)) for i in range(2)]
            vst = [_e(nc.sbuf_tensor(f"vst{i}", [P, H * 65], BF16)) for i in range(2)]
            pt = [_e(nc.psum_tensor(f"pt{i}", [P, 8 * P], BF16)) for i in range(2)]
            pk = [_e(nc.psum_tensor(f"pk{i}", [P, 512], F32)) for i in range(2)]
            pv = [_e(nc.psum_tensor(f"pv{i}", [P, 512], F32)) for i in range(2)]

            ve.memset(
                vst[0][:].rearrange("p (h c) -> p h c", c=65)[:, :, 64:65], 1.0)
            ve.memset(
                vst[1][:].rearrange("p (h c) -> p h c", c=65)[:, :, 64:65], 1.0)
            c_vones = s_v.n

            rel_xf_sc = [0, 0]
            rel_xf_v = [0, 0]
            rel_xn = [0, 0, 0, 0]
            rel_xT = [0, 0]
            rel_kst = [0, 0]
            rel_vst = [c_vones, c_vones]
            rel_vst_gd = [0, 0]
            tr_rel = [0, 0]
            kev = [0, 0]
            vev = [0, 0]
            prev_sb_reader = 0

            pid_rv = nc.partition_id([mybir.EngineType.Pool])
            rb = gp.scalar_reg_alu(ALU.mod, pid_rv, 2)
            rkbase = gp.scalar_reg_alu(ALU.mult, rb, D)
            rvbase = gp.scalar_reg_alu(ALU.mult, rb, S)
            ve.memset(flgsb[:], 0.0)

            for ch in range(NT * P // 512):
                tloc = ch
                cpar = ch % 2
                for tt in range(4):
                    i = ch * 4 + tt
                    if tt == 0:
                        W(sy, s_sc, rel_xf_sc[0])
                        W(sy, s_v, rel_xf_v[0])
                    ins = sy.dma_start(xf[tt][:], x_kv[i * P:(i + 1) * P, :])
                    c_load4 = s_x[cpar].inc(ins, 16)
                for tt in range(4):
                    stats_acc(xf[tt][:], s_x[cpar], c_load4, tt)
                c_acc = s_sc.n
                W(ve, s_sc, c_acc)
                stats_finalize(4, None)
                rel_xf_sc[0] = s_sc.n
                for tt in range(4):
                    W(ve, s_t, rel_xn[tt])
                    ve.tensor_scalar(xn[tt][:], xf[tt][:],
                                     s_s1[:, 8 + tt:9 + tt],
                                     s_s1[:, 12 + tt:13 + tt],
                                     ALU.subtract, ALU.mult)
                c_norm = s_v.n
                rel_xf_v[0] = s_v.n
                for tt in range(4):
                    bk = tt % 2
                    W(te, s_v, c_norm)
                    W(te, s_v, tr_rel[bk])
                    W(te, s_t, rel_xT[cpar])
                    for dc in range(8):
                        c_tp = s_t.inc(te.transpose(pt[bk][:, dc * P:(dc + 1) * P],
                                                    xn[tt][:, dc * P:(dc + 1) * P],
                                                    ident_sb[:]))
                    rel_xn[tt] = s_t.n
                    W(ve, s_t, c_tp)
                    ve.tensor_copy(
                        xT[cpar][:].rearrange("p (c f) -> p c f", f=512)
                        [:, :, tt * P:(tt + 1) * P],
                        pt[bk][:].rearrange("p (c q) -> p c q", q=P))
                    tr_rel[bk] = s_v.n
                c_xT = s_v.n

                W(te, s_wq, c_wq)
                for ec in range(8):
                    kp = ec % 2
                    W(te, s_v, c_xT)
                    W(te, s_v, kev[kp])
                    for dc in range(8):
                        mm = te.matmul(
                            pk[kp][:],
                            wq_sb[:, dc * E3 + D + ec * P: dc * E3 + D + (ec + 1) * P],
                            xT[cpar][:, dc * 512:(dc + 1) * 512],
                            start=(dc == 0), stop=(dc == 7))
                    c_kmm = s_t.inc(mm)
                    W(ve, s_t, c_kmm)
                    W(ve, s_sk[kp], rel_kst[kp])
                    ve.tensor_copy(kst[kp][:], pk[kp][:])
                    kev[kp] = s_v.n
                    W(gp, s_v, kev[kp])
                    ins = gp.dma_start(
                        kT_loc[ec * P:(ec + 1) * P, tloc * 512:(tloc + 1) * 512],
                        kst[kp][:])
                    rel_kst[kp] = s_sk[kp].inc(ins, 16)

                for tt in range(4):
                    vp = tt % 2
                    for nh in range(2):
                        W(te, s_v, c_xT)
                        W(te, s_v, vev[vp])
                        for dc in range(8):
                            mm = te.matmul(
                                pv[vp][:],
                                xT[cpar][:, dc * 512 + tt * P: dc * 512 + (tt + 1) * P],
                                wq_sb[:, dc * E3 + 2 * D + nh * 512:
                                      dc * E3 + 2 * D + (nh + 1) * 512],
                                start=(dc == 0), stop=(dc == 7))
                        c_vmm = s_t.inc(mm)
                        W(ve, s_t, c_vmm)
                        W(ve, s_sv[vp], rel_vst_gd[vp])
                        ve.tensor_copy(
                            vst[vp][:].rearrange("p (h c) -> p h c", c=65)
                            [:, nh * 8:(nh + 1) * 8, 0:64],
                            pv[vp][:].rearrange("p (h e) -> p h e", e=64))
                        vev[vp] = s_v.n
                    W(gp, s_v, vev[vp])
                    ins = gp.dma_start(
                        v_loc[(tloc * 4 + tt) * P:(tloc * 4 + tt + 1) * P, :],
                        vst[vp][:])
                    rel_vst_gd[vp] = s_sv[vp].inc(ins, 16)
                rel_xT[cpar] = s_t.n

        c_kvdone = (s_sk[0].n, s_sk[1].n, s_sv[0].n, s_sv[1].n)
        W(gp, s_sk[0], c_kvdone[0])
        W(gp, s_sk[1], c_kvdone[1])
        W(gp, s_sv[0], c_kvdone[2])
        W(gp, s_sv[1], c_kvdone[3])
        ins = gp.dma_start(kT_sh[bass.ds(rkbase, D)], kT_loc)
        s_bulk.inc(ins, 16)
        ins = gp.dma_start(v_sh[bass.ds(rvbase, S)], v_loc)
        c_bulk = s_bulk.inc(ins, 16)
        gp.wait_ge(s_bulk.h, c_bulk)
        gp.wait_ge(s_c.h, c_consts)
        s_flg.inc(gp.dma_start(kvflag_sh[bass.ds(rb, 1), :],
                               ones_sb[:, 0:8]), 16)
        phase_join()
        with ExitStack() as _qst:
            _e = _qst.enter_context
            qf = [_e(nc.sbuf_tensor(f"qf{i}", [P, D], F32)) for i in range(4)]
            scr["b"] = _e(nc.sbuf_tensor("scrQb", [P, D], F32))
            qn = [_e(nc.sbuf_tensor(f"qn{i}", [P, D], BF16)) for i in range(4)]
            qTb = [_e(nc.sbuf_tensor(f"qTb{i}", [P, 8 * 512], BF16)) for i in range(2)]
            qt = [_e(nc.psum_tensor(f"qt{i}", [P, 8 * P], BF16)) for i in range(2)]
            pq = [_e(nc.psum_tensor(f"pq{i}", [P, 512], F32)) for i in range(2)]
            rel_qf_sc = [s_sc.n, s_sc.n]
            rel_qf_v = [s_v.n, s_v.n]
            rel_qn = [s_t.n] * 4
            rel_qT = [s_t.n, s_t.n]
            tr_rel = [s_v.n, s_v.n]
            qev = [0, 0]
            prev_sb_reader = s_v.n
            for chq in range(4):
                cpar = chq % 2
                for tt in range(4):
                    slot = chq * 4 + tt
                    if tt == 0:
                        W(sy, s_sc, rel_qf_sc[0])
                        W(sy, s_v, rel_qf_v[0])
                    ins = sy.dma_start(qf[tt][:], x_q[slot * P:(slot + 1) * P, :])
                    c_load4 = s_x[cpar].inc(ins, 16)
                for tt in range(4):
                    stats_acc(qf[tt][:], s_x[cpar], c_load4, tt)
                c_acc = s_sc.n
                W(ve, s_sc, c_acc)
                stats_finalize(4, chq * 4)
                rel_qf_sc[0] = s_sc.n
                for tt in range(4):
                    W(ve, s_t, rel_qn[tt])
                    ve.tensor_scalar(qn[tt][:], qf[tt][:],
                                     s_s1[:, 8 + tt:9 + tt],
                                     s_s1[:, 12 + tt:13 + tt],
                                     ALU.subtract, ALU.mult)
                c_norm = s_v.n
                rel_qf_v[0] = s_v.n
                for tt in range(4):
                    bk = tt % 2
                    W(te, s_v, c_norm)
                    W(te, s_v, tr_rel[bk])
                    W(te, s_t, rel_qT[cpar])
                    for dc in range(8):
                        c_tp = s_t.inc(te.transpose(qt[bk][:, dc * P:(dc + 1) * P],
                                                    qn[tt][:, dc * P:(dc + 1) * P],
                                                    ident_sb[:]))
                    rel_qn[tt] = s_t.n
                    W(ve, s_t, c_tp)
                    ve.tensor_copy(
                        qTb[cpar][:].rearrange("p (c f) -> p c f", f=512)
                        [:, :, tt * P:(tt + 1) * P],
                        qt[bk][:].rearrange("p (c q) -> p c q", q=P))
                    tr_rel[bk] = s_v.n
                c_qT = s_v.n
                for ec in range(8):
                    qp = ec % 2
                    W(te, s_v, c_qT)
                    W(te, s_v, qev[qp])
                    for dc in range(8):
                        mm = te.matmul(
                            pq[qp][:],
                            wq_sb[:, dc * E3 + ec * P: dc * E3 + (ec + 1) * P],
                            qTb[cpar][:, dc * 512:(dc + 1) * 512],
                            start=(dc == 0), stop=(dc == 7))
                    c_qmm = s_t.inc(mm)
                    W(ve, s_t, c_qmm)
                    ve.tensor_scalar(
                        qT_sb[:, ec * TCORE + chq * 512: ec * TCORE + (chq + 1) * 512],
                        pq[qp][:], 0.125, None, ALU.mult)
                    qev[qp] = s_v.n
                rel_qT[cpar] = s_t.n

    phase_join()
    with ExitStack() as _fst:
        _e = _fst.enter_context
        bfc_hl = _e(nc.sbuf_tensor("bfc_hl", [2, HID], BF16))
        bpr_hl = _e(nc.sbuf_tensor("bpr_hl", [2, D], BF16))
        ones2 = _e(nc.sbuf_tensor("ones2", [2, P], BF16))
        bfrow = _e(nc.sbuf_tensor("bfrow", [1, D], F32))
        blo = _e(nc.sbuf_tensor("blo", [1, D], BF16))
        fx = [_e(nc.sbuf_tensor(f"fx{i}", [P, D], F32)) for i in range(2)]
        fxn = _e(nc.sbuf_tensor("fxn", [P, D], F32))
        scr["a"] = _e(nc.sbuf_tensor("scrFa", [P, D], F32))
        fxq = [_e(nc.sbuf_tensor(f"fxq{i}", [P, D], BF16)) for i in range(2)]
        xqT = _e(nc.sbuf_tensor("xqT", [P, 2 * D], BF16))
        wfcb = [_e(nc.sbuf_tensor(f"wfcb{i}", [P, 8 * 512], BF16)) for i in range(2)]
        hgl = [_e(nc.sbuf_tensor(f"hgl{i}", [P, 1024], F32)) for i in range(2)]
        hqb = [_e(nc.sbuf_tensor(f"hqb{i}", [P, 1024], BF16)) for i in range(2)]
        hqT = _e(nc.sbuf_tensor("hqT", [P, 32 * 256], BF16))
        wprb = [_e(nc.sbuf_tensor(f"wprb{i}", [P, D], BF16)) for i in range(2)]

        ve.memset(ones2[:], 1.0)
        c_b_gate = 0
        c_lo = 0
        for bsrc, hl, width in ((b_fc, bfc_hl, HID), (b_proj, bpr_hl, D)):
            for qo in range(0, width, D):
                W(sy, s_v, c_b_gate)
                ins = sy.dma_start(bfrow[:], bsrc[:, qo:qo + D])
                c_br = s_br.inc(ins, 16)
                W(ve, s_br, c_br)
                W(ve, s_wq, c_lo)
                ve.tensor_copy(hl[0:1, qo:qo + D], bfrow[:])
                ve.tensor_copy(scr["a"][0:1, :], hl[0:1, qo:qo + D])
                ve.tensor_tensor(blo[:], bfrow[:], scr["a"][0:1, :],
                                 ALU.subtract)
                c_b_gate = s_v.n
                W(gp, s_v, c_b_gate)
                ins = gp.dma_start(hl[1:2, qo:qo + D], blo[:])
                c_lo = s_wq.inc(ins, 16)
        c_bias_ready = s_v.n

        with ExitStack() as _fps:
            _e2 = _fps.enter_context
            pf = [_e2(nc.psum_tensor(f"pf{i}", [P, 1024], F32)) for i in range(2)]
            pp2 = _e2(nc.psum_tensor("pp2", [P, 2048], F32))
            trb = [pp2[:, 0:512].bitcast(BF16), pp2[:, 512:1024].bitcast(BF16)]
            tr_rel = [s_v.n, s_v.n]
            rel_fx_v = [s_v.n, s_v.n]
            rel_fxq = [s_t.n, s_t.n]
            fev = [0, 0]
            rel_wfc = [s_t.n, s_t.n]
            rel_wpr = [s_t.n, s_t.n]
            c_wfc_ld = [0, 0]
            c_wpr_ld = [0, 0]
            rel_hgl_v = [0, 0]
            c_gelu = [0, 0]
            rel_hqb_t = [s_t.n, s_t.n]
            rel_xqT = s_t.n
            rel_hqT = s_t.n
            c_pp_ev = 0

            for sc0 in range(8):
                for tt in range(2):
                    slot = sc0 * 2 + tt
                    par = tt % 2
                    W(sy, s_v, rel_fx_v[par])
                    ins = sy.dma_start(fx[par][:], x_q[slot * P:(slot + 1) * P, :])
                    c_load = s_x[par].inc(ins, 16)
                    W(ve, s_x[par], c_load)
                    ve.tensor_scalar(fxn[:], fx[par][:],
                                     mr_sb[:, 2 * slot:2 * slot + 1],
                                     mr_sb[:, 2 * slot + 1:2 * slot + 2],
                                     ALU.subtract, ALU.mult)
                    rel_fx_v[par] = s_v.n
                    W(ve, s_t, rel_fxq[par])
                    quant_free(fxq[par][:], fxn[:], D)
                    c_q = s_v.n
                    bk = tt % 2
                    W(te, s_v, c_q)
                    W(te, s_v, tr_rel[bk])
                    W(te, s_v, c_pp_ev)
                    W(te, s_t, rel_xqT)
                    for dc in range(8):
                        c_tp = s_t.inc(te.transpose(trb[bk][:, dc * P:(dc + 1) * P],
                                                    fxq[par][:, dc * P:(dc + 1) * P],
                                                    ident_sb[:]))
                    W(ve, s_t, c_tp)
                    ve.tensor_copy(xqT[:, tt * D:(tt + 1) * D], trb[bk][:])
                    tr_rel[bk] = s_v.n
                    rel_fxq[par] = s_t.n
                c_xqT = s_v.n

                def emit_hq_transposes(pn, pfp, c_hq_p):
                    bk3 = pn % 2
                    W(te, s_v, c_hq_p)
                    W(te, s_v, tr_rel[bk3])
                    c_tp = None
                    for j in range(8):
                        c_tp = s_t.inc(te.transpose(
                            trb[bk3][:, j * P:(j + 1) * P],
                            hqb[pfp][:, j * P:(j + 1) * P],
                            ident_sb[:]))
                    rel_hqb_t[pfp] = s_t.n
                    W(ve, s_t, c_tp)
                    W(ve, s_t, rel_hqT)
                    for tt in range(2):
                        ve.tensor_copy(
                            hqT[:].rearrange("p (c f) -> p c f", f=256)
                            [:, pn * 4:(pn + 1) * 4, tt * P:(tt + 1) * P],
                            trb[bk3][:, tt * 512:(tt + 1) * 512]
                            .rearrange("p (c q) -> p c q", q=P))
                    tr_rel[bk3] = s_v.n

                pending = None
                for nch in range(8):
                    wp = nch % 2
                    fp_ = nch % 2
                    W(sy, s_t, rel_wfc[wp])
                    ins = sy.dma_start(
                        wfcb[wp][:].rearrange("p (c n) -> p c n", n=512),
                        wfcqT_d.rearrange("(c p) n -> p c n", p=P)
                        [:, :, nch * 512:(nch + 1) * 512])
                    c_wfc_ld[wp] = s_wf[wp].inc(ins, 16)
                    W(te, s_v, c_xqT)
                    W(te, s_wf[wp], c_wfc_ld[wp])
                    W(te, s_sc, fev[fp_])
                    if sc0 == 0 and nch == 0:
                        W(te, s_v, c_bias_ready)
                        W(te, s_wq, c_lo)
                    for tt in range(2):
                        for dc in range(8):
                            te.matmul(
                                pf[fp_][:, tt * 512:(tt + 1) * 512],
                                xqT[:, tt * D + dc * P: tt * D + (dc + 1) * P],
                                wfcb[wp][:, dc * 512:(dc + 1) * 512],
                                start=(dc == 0), stop=False)
                        mm = te.matmul(
                            pf[fp_][:, tt * 512:(tt + 1) * 512],
                            ones2[:], bfc_hl[:, nch * 512:(nch + 1) * 512],
                            start=False, stop=True)
                    c_fmm = s_t.inc(mm)
                    rel_wfc[wp] = s_t.n
                    if pending is not None:
                        emit_hq_transposes(*pending)
                    W(sc, s_t, c_fmm)
                    W(sc, s_v, rel_hgl_v[fp_])
                    sc.activation(hgl[fp_][:], pf[fp_][:], AF.Gelu)
                    c_gelu[fp_] = s_sc.n
                    fev[fp_] = c_gelu[fp_]
                    W(ve, s_sc, c_gelu[fp_])
                    W(ve, s_t, rel_hqb_t[fp_])
                    quant_free(hqb[fp_][:], hgl[fp_][:], 1024)
                    c_hq = s_v.n
                    rel_hgl_v[fp_] = c_hq
                    pending = (nch, fp_, c_hq)
                emit_hq_transposes(*pending)
                c_hqT = s_v.n

                for ncr in range(32):
                    wp = ncr % 2
                    W(sy, s_t, rel_wpr[wp])
                    ins = sy.dma_start(wprb[wp][:],
                                       wprojqT_d[ncr * P:(ncr + 1) * P, :])
                    c_wpr_ld[wp] = s_wp[wp].inc(ins, 16)
                    W(te, s_wp[wp], c_wpr_ld[wp])
                    if ncr == 0:
                        W(te, s_v, c_hqT)
                        W(te, s_v, tr_rel[0])
                        W(te, s_v, tr_rel[1])
                        for tt in range(2):
                            for mh in range(2):
                                te.matmul(
                                    pp2[:, tt * 1024 + mh * 512:
                                        tt * 1024 + (mh + 1) * 512],
                                    ones2[:], bpr_hl[:, mh * 512:(mh + 1) * 512],
                                    start=True, stop=False,
                                    skip_group_check=True)
                    for tt in range(2):
                        for mh in range(2):
                            mm = te.matmul(
                                pp2[:, tt * 1024 + mh * 512:
                                    tt * 1024 + (mh + 1) * 512],
                                hqT[:, ncr * 256 + tt * P:
                                    ncr * 256 + (tt + 1) * P],
                                wprb[wp][:, mh * 512:(mh + 1) * 512],
                                start=False, stop=(ncr == 31),
                                skip_group_check=True)
                    rel_wpr[wp] = s_t.inc(mm)
                c_proj = s_t.n
                W(ve, s_t, c_proj)
                for rr in range(4):
                    ve.tensor_copy(
                        ffn_sb[:, sc0 * 2 * D + rr * 512:
                               sc0 * 2 * D + (rr + 1) * 512],
                        pp2[:, rr * 512:(rr + 1) * 512])
                c_pp_ev = s_v.n
                rel_xqT = s_t.n
                rel_hqT = s_t.n

    phase_join()
    rpn = gp.scalar_reg_alu(ALU.add, pid_rv, 1)
    rr_ = gp.scalar_reg_alu(ALU.mod, rpn, 2)
    if PAIR_POLL:
        for _ in range(400):
            s_flg.inc(gp.dma_start(flgsb[:], kvflag_sh[bass.ds(rr_, 1), :]),
                      16)
        rf = gp.alloc_register()
        gp.reg_load(rf, flgsb[0:1, 0:1].bitcast(U32))
        with gp.While(rf == 0):
            gp.reg_load(rf, flgsb[0:1, 0:1].bitcast(U32))
        gp.sem_inc(s_pr.h, 1)
    else:
        ins = gp.dma_start(flgsb[:], kvflag_sh[bass.ds(rr_, 1), :])
        c_fl = s_flg.inc(ins, 16)
        gp.wait_ge(s_flg.h, c_fl)
        gp.sem_inc(s_pr.h, 1)
    with ExitStack() as _ast:
        _e = _ast.enter_context
        kbuf = [_e(nc.sbuf_tensor(f"kbuf{i}", [P, S], BF16)) for i in range(2)]
        kraw = _e(nc.sbuf_tensor("kraw", [P, S], BF16))
        masks_sb = _e(nc.sbuf_tensor("masks_sb", [P, NT * P], BF16))
        ld = sy.dma_start(masks_sb[:], masks)
        c_masks = s_c.inc(ld, 16)
        vbuf = [_e(nc.sbuf_tensor(f"vbuf{i}", [P, NT * 130], BF16)) for i in range(2)]
        pT = [_e(nc.sbuf_tensor(f"pTb{i}", [P, 2048], BF16)) for i in range(3)]
        pTd = [_e(nc.sbuf_tensor(f"pTd{i}", [P, 256], BF16)) for i in range(2)]
        sps = _e(nc.psum_tensor("sps", [P, 2048], F32))
        avp = _e(nc.psum_tensor("avp", [P, 2048], F32))

        def av_off(sh):
            return 128 * sh

        c_kvload = [0, 0]
        rel_kv = [0, 0]
        c_avev = 0
        c_exp = {}
        c_mask = {}
        c_avt = {}
        dg_rel0 = 0

        def issue_kv_load(bhp, rel_kraw_t):
            b, hp = bhp // 8, bhp % 8
            par = bhp % 2
            W(sy, s_t, rel_kv[par])
            W(sy, s_t, rel_kraw_t)
            W(sy, s_sk[0], c_kvdone[0])
            W(sy, s_sk[1], c_kvdone[1])
            W(sy, s_sv[0], c_kvdone[2])
            W(sy, s_sv[1], c_kvdone[3])
            kview = kT_sh[b * D:(b + 1) * D].rearrange("c (j dd) -> j c dd",
                                                       dd=64)
            for cg in range(8):
                ins = sy.dma_start(
                    kraw[:].rearrange("j (c dd) -> j c dd", dd=64)
                    [:, cg * 16:(cg + 1) * 16, :],
                    kview[:, hp * P + cg * 16: hp * P + (cg + 1) * 16, :])
                s_kv[par].inc(ins, 16)
            c = 0
            for g in range(8):
                ins = sy.dma_start(
                    vbuf[par][:, g * 1040:(g + 1) * 1040]
                    .rearrange("p (k c) -> p k c", c=130),
                    v_sh[b * S:(b + 1) * S]
                    .rearrange("(g k p) c -> g p k c", p=P, k=8)
                    [g, :, :, 2 * hp * 65: 2 * hp * 65 + 130])
                c = s_kv[par].inc(ins, 16)
            c_kvload[par] = c

        ve.wait_ge(s_c.h, c_masks)
        sy.wait_ge(s_pr.h, 1)
        sy.wait_ge(s_bulk.h, c_bulk)
        rel_kraw = 0
        c_exp_last = 0
        issue_kv_load(0, 0)
        for bhp in range(16):
            b, hp = bhp // 8, bhp % 8
            par = bhp % 2

            W(te, s_kv[par], c_kvload[par])
            W(te, s_sc, c_exp_last)
            trb = [sps[:, 0:512].bitcast(BF16), sps[:, 1024:1536].bitcast(BF16)]
            c_trev = {-2: 0, -1: 0}
            for g in range(16):
                bk = g % 2
                W(te, s_v, c_trev[g - 2])
                for i in range(4):
                    kt = g * 4 + i
                    for hl in range(2):
                        mm = te.transpose(
                            trb[bk][hl * 64:(hl + 1) * 64, i * P:(i + 1) * P],
                            kraw[:, (hl * 64 + kt) * 64:(hl * 64 + kt + 1) * 64],
                            ident_sb[:])
                c_tr = s_t.inc(mm)
                W(ve, s_t, c_tr)
                ve.tensor_copy(kbuf[par][:, g * 512:(g + 1) * 512],
                               trb[bk][:, 0:512])
                c_trev[g] = s_v.n
            rel_kraw = s_t.n
            c_kbuf = s_v.n

            if bhp + 1 < 16:
                issue_kv_load(bhp + 1, rel_kraw)
            c_exp.clear()
            c_mask.clear()
            c_avt.clear()
            for dd in range(1, 5):
                c_exp[-dd] = c_exp_last
                c_avt[-dd] = s_t.n
            md = [0]
            pTd_idx = {}
            dg_rel = [dg_rel0, dg_rel0]
            avp_started = set()
            qbase = hp * TCORE + b * 1024

            stages = []
            for kt in range(NT):
                c0 = (kt // 8) * P
                if c0 < 512:
                    stages.append((kt, c0, 512))
                    stages.append((kt, 512, 1024))
                else:
                    stages.append((kt, c0, 1024))
            nstg = len(stages)

            def _av_mm(si, sp, kt, hh, cl):
                sh = hh * 8 + cl
                off = av_off(sh)
                bk = sh // 4
                st = bk not in avp_started
                avp_started.add(bk)
                return te.matmul(
                    avp[:, off:off + 65],
                    pT[sp][:, hh * 1024 + cl * P: hh * 1024 + (cl + 1) * P],
                    vbuf[par][:, kt * 130 + hh * 65: kt * 130 + (hh + 1) * 65],
                    start=st, stop=(kt == 8 * (cl + 1) - 1),
                    skip_group_check=True)

            def emit_av_nd(si):
                kt, a0, a1 = stages[si]
                sp = si % 3
                start = kt // 8
                diag = a0 <= start * P < a1
                W(te, s_sc, c_exp[si])
                mm = None
                for hh in range(2):
                    for cl in range(max(start, a0 // P), a1 // P):
                        if diag and cl == start:
                            continue
                        mm = _av_mm(si, sp, kt, hh, cl)
                c_avt[si] = s_t.inc(mm) if mm is not None else s_t.n

            def emit_av_dg(si):
                kt, a0, a1 = stages[si]
                start = kt // 8
                if not (a0 <= start * P < a1):
                    return
                W(te, s_v, c_mask[si])
                pdx = pTd_idx[si]
                for hh in range(2):
                    sh = hh * 8 + start
                    off = av_off(sh)
                    bk = sh // 4
                    st = bk not in avp_started
                    avp_started.add(bk)
                    mm = te.matmul(
                        avp[:, off:off + 65],
                        pTd[pdx][:, hh * P:(hh + 1) * P],
                        vbuf[par][:, kt * 130 + hh * 65: kt * 130 + (hh + 1) * 65],
                        start=st, stop=(kt == 8 * (start + 1) - 1),
                        skip_group_check=True)
                dg_rel[pdx] = s_t.inc(mm)

            for si in range(nstg):
                kt, a0, a1 = stages[si]
                sp2 = si % 2
                sp = si % 3
                start = kt // 8
                lo = a0 % 512
                width = a1 - a0
                W(te, s_kv[par], c_kvload[par])
                W(te, s_sc, c_exp[si - 2])
                if si == 0:
                    W(te, s_v, c_avev)
                    W(te, s_v, c_kbuf)
                for hh in range(2):
                    mm = te.matmul(
                        sps[:, hh * 1024 + sp2 * 512 + lo:
                            hh * 1024 + sp2 * 512 + lo + width],
                        kbuf[par][hh * 64:(hh + 1) * 64, kt * P:(kt + 1) * P],
                        qT_sb[hh * 64:(hh + 1) * 64, qbase + a0: qbase + a1],
                        start=True, stop=True)
                c_qk = s_t.inc(mm)
                if si >= 1:
                    emit_av_nd(si - 1)
                    emit_av_dg(si - 1)
                W(sc, s_t, c_qk)
                W(sc, s_t, c_avt[si - 3])
                src = sps[:].rearrange("p (h q) -> p h q", h=2)\
                    [:, :, sp2 * 512 + lo: sp2 * 512 + lo + width]
                dst = pT[sp][:].rearrange("p (h q) -> p h q", h=2)[:, :, a0:a1]
                sc.activation(dst, src, AF.Exp, bias=expb[:], scale=1.0)
                c_exp[si] = s_sc.n
                c_exp_last = c_exp[si]
                if a0 <= start * P < a1:
                    pdx = md[0] % 2
                    md[0] += 1
                    pTd_idx[si] = pdx
                    W(ve, s_sc, c_exp[si])
                    W(ve, s_t, dg_rel[pdx])
                    ve.tensor_tensor(
                        pTd[pdx][:].rearrange("p (h q) -> p h q", h=2),
                        pT[sp][:].rearrange("p (h q) -> p h q", h=2)
                        [:, :, start * P:(start + 1) * P],
                        masks_sb[:, kt * P:(kt + 1) * P]
                        .unsqueeze(1).broadcast_to((P, 2, P)),
                        ALU.mult)
                    c_mask[si] = s_v.n

            emit_av_nd(nstg - 1)
            emit_av_dg(nstg - 1)
            c_av_last = s_t.n
            dg_rel0 = s_t.n
            rel_kv[par] = c_av_last

            W(ve, s_t, c_av_last)
            ve.reciprocal(
                s_s1[:, 0:16].unsqueeze(2),
                avp[:].rearrange("p (s c) -> p s c", c=128)[:, :, 64:65])
            dst = attn_sb[:].rearrange("p (s hh e) -> p s hh e", hh=16, e=64)\
                [:, b * 8:(b + 1) * 8, 2 * hp:2 * hp + 2, :]
            src = avp[:].rearrange("p (hh cl c) -> p cl hh c", hh=2, c=128)\
                [:, :, :, 0:64]
            rcp = s_s1[:, 0:16].rearrange("p (hh cl) -> p cl hh", hh=2)\
                .unsqueeze(3).broadcast_to((P, 8, 2, 64))
            ve.tensor_tensor(dst, src, rcp, ALU.mult)
            c_avev = s_v.n

    phase_join()
    with ExitStack() as _rst:
        _e = _rst.enter_context
        rx = [_e(nc.sbuf_tensor(f"rx{i}", [P, D], F32)) for i in range(2)]
        ro = [_e(nc.sbuf_tensor(f"ro{i}", [P, D], F32)) for i in range(2)]
        scr["a"] = _e(nc.sbuf_tensor("scrRa", [P, D], F32))
        scr["b"] = _e(nc.sbuf_tensor("scrRb", [P, D], F32))
        rel_rx = [s_v.n, s_v.n]
        rel_ro = [0, 0]
        for slot in range(TPC):
            par = slot % 2
            W(sy, s_v, rel_rx[par])
            ins = sy.dma_start(rx[par][:], x_q[slot * P:(slot + 1) * P, :])
            c_load = s_x[par].inc(ins, 16)
            W(ve, s_x[par], c_load)
            W(ve, s_y[par], rel_ro[par])
            ve.tensor_copy(scr["a"][:, 0:D], attn_sb[:, slot * D:(slot + 1) * D])
            ve.tensor_copy(scr["b"][:, 0:D], ffn_sb[:, slot * D:(slot + 1) * D])
            ve.tensor_tensor(ro[par][:], rx[par][:], scr["a"][:, 0:D], ALU.add)
            ve.tensor_tensor(ro[par][:], ro[par][:], scr["b"][:, 0:D], ALU.add)
            c_ro = s_v.n
            rel_rx[par] = c_ro
            W(sy, s_v, c_ro)
            ins = sy.dma_start(y[slot * P:(slot + 1) * P, :], ro[par][:])
            rel_ro[par] = s_y[par].inc(ins, 16)
        sy.wait_ge(s_y[0].h, s_y[0].n)
        sy.wait_ge(s_y[1].h, s_y[1].n)

    return nc


_CACHE = {}


def _masks_for_core(c):
    m = np.zeros((P, NT * P), dtype=np.float32)
    kk = np.arange(P)[:, None]
    qq = np.arange(P)[None, :]
    for kt in range(NT):
        j = 8 * (kt // 8) + c
        m[:, kt * P:(kt + 1) * P] = ((kt * P + kk) <= (j * P + qq))
    return m.astype(ml_dtypes.bfloat16)


def _slot_rows(c):
    rows = []
    for b in range(B):
        for k in range(NCLS):
            j = 8 * k + c
            base = b * S + j * P
            rows.append((base, base + P))
    return rows


def kernel(L0_attn_x, L0_attn_qkv_weights, L0_ffn_w_fc, L0_ffn_b_fc,
           L0_ffn_w_proj, L0_ffn_b_proj):
    x = np.ascontiguousarray(np.asarray(L0_attn_x, dtype=np.float32).reshape(B * S, D))
    wqkv = np.ascontiguousarray(np.asarray(L0_attn_qkv_weights, dtype=np.float32))
    wfc = np.ascontiguousarray(np.asarray(L0_ffn_w_fc, dtype=np.float32))
    bfc = np.ascontiguousarray(np.asarray(L0_ffn_b_fc, dtype=np.float32).reshape(1, HID))
    wproj = np.ascontiguousarray(np.asarray(L0_ffn_w_proj, dtype=np.float32))
    bproj = np.ascontiguousarray(np.asarray(L0_ffn_b_proj, dtype=np.float32).reshape(1, D))

    if "nc" not in _CACHE:
        _CACHE["nc"] = build()
    nc = _CACHE["nc"]

    in_maps = []
    for c in range(NCORES):
        rows = _slot_rows(c)
        x_qc = np.concatenate([x[a:b_] for (a, b_) in rows], axis=0)
        in_maps.append({
            "x_kv": np.ascontiguousarray(x[(c % 2) * S:(c % 2 + 1) * S]),
            "x_q": np.ascontiguousarray(x_qc),
            "w_qkv": wqkv, "w_fc": wfc, "b_fc": bfc,
            "w_proj": wproj, "b_proj": bproj,
            "masks": _masks_for_core(c),
        })
    res = run_bass_kernel_spmd(nc, in_maps, core_ids=list(range(NCORES)))
    out = np.empty((B * S, D), dtype=np.float32)
    for c in range(NCORES):
        yc = res.results[c]["y"]
        for si, (a, b_) in enumerate(_slot_rows(c)):
            out[a:b_] = yc[si * P:(si + 1) * P]
    return out.reshape(B, S, D)


# version 23
# speedup vs baseline: 1.2951x; 1.2951x over previous
from contextlib import ExitStack

import numpy as np
import ml_dtypes
import concourse.bass as bass
import concourse.mybir as mybir
from concourse.bass_utils import run_bass_kernel_spmd

F32 = mybir.dt.float32
BF16 = mybir.dt.bfloat16
U32 = mybir.dt.uint32
AF = mybir.ActivationFunctionType
ALU = mybir.AluOpType
AX = mybir.AxisListType

NCORES = 8
P = 128
D = 1024
S = 8192
B = 2
H = 16
DH = 64
E3 = 3 * D
HID = 4096
NT = S // P
NCLS = 8
TPC = 2 * NCLS
TCORE = TPC * P
EXP_SHIFT = -25.0
PAIR_POLL = True
SHARED_KV = True
AMAX_CLAMP = 2.0 ** -96
MAGIC = float(1.5 * 2.0 ** 20)
EXPO_MASK = 0x7F800000


class Sem:
    def __init__(self, nc, name):
        self.h = nc.alloc_semaphore(name)
        self.n = 0

    def inc(self, ins, k=1):
        ins.then_inc(self.h, k)
        self.n += k
        return self.n


class AutoChain:

    _PASS = {"wait_ge", "engine", "bass"}

    def __init__(self, eng, sem):
        self._eng = eng
        self._sem = sem

    def __getattr__(self, name):
        f = getattr(self._eng, name)
        if name in AutoChain._PASS or not callable(f):
            return f
        sem = self._sem
        eng = self._eng

        def wrap(*a, **k):
            if sem.n:
                eng.wait_ge(sem.h, sem.n)
            ins = f(*a, **k)
            sem.inc(ins)
            return ins

        return wrap


def build():
    nc = bass.Bass("TRN2", target_bir_lowering=False)

    x_kv = nc.dram_tensor("x_kv", [S, D], F32, kind="ExternalInput").ap()
    x_q = nc.dram_tensor("x_q", [TCORE, D], F32, kind="ExternalInput").ap()
    w_qkv = nc.dram_tensor("w_qkv", [D, E3], F32, kind="ExternalInput").ap()
    w_fc = nc.dram_tensor("w_fc", [HID, D], F32, kind="ExternalInput").ap()
    b_fc = nc.dram_tensor("b_fc", [1, HID], F32, kind="ExternalInput").ap()
    w_proj = nc.dram_tensor("w_proj", [D, HID], F32, kind="ExternalInput").ap()
    b_proj = nc.dram_tensor("b_proj", [1, D], F32, kind="ExternalInput").ap()
    masks = nc.dram_tensor("masks", [P, 8 * P], BF16, kind="ExternalInput").ap()
    y = nc.dram_tensor("y", [TCORE, D], F32, kind="ExternalOutput").ap()

    _shr = {"addr_space": "Shared"} if SHARED_KV else {}
    kT_sh = nc.dram_tensor("kT_sh", [B * D, S], BF16, kind="Internal",
                           **_shr).ap()
    v_sh = nc.dram_tensor("v_sh", [B * S, H * 65], BF16, kind="Internal",
                          **_shr).ap()
    kT_loc = nc.dram_tensor("kT_loc", [D, S], BF16, kind="Internal").ap()
    v_loc = nc.dram_tensor("v_loc", [S, H * 65], BF16, kind="Internal").ap()
    kvflag_sh = nc.dram_tensor("kvflag_sh", [2, 8], F32, kind="Internal",
                               **_shr).ap()
    wfcqT_d = nc.dram_tensor("wfcqT_d", [D, HID], BF16, kind="Internal").ap()
    wprojqT_d = nc.dram_tensor("wprojqT_d", [HID, D], BF16, kind="Internal").ap()

    ident_d = nc.inline_tensor(np.eye(P, dtype=ml_dtypes.bfloat16), name="idbf").ap()
    ones_d = nc.inline_tensor(np.ones((1, P), np.float32), name="onef").ap()

    s_c = Sem(nc, "s_c")
    s_x = [Sem(nc, "s_x0"), Sem(nc, "s_x1")]
    s_wr = Sem(nc, "s_wr")
    s_br = Sem(nc, "s_br")
    s_wf = [Sem(nc, "s_wf0"), Sem(nc, "s_wf1"), Sem(nc, "s_wf2")]
    s_wp = [Sem(nc, "s_wp0"), Sem(nc, "s_wp1"), Sem(nc, "s_wp2"),
            Sem(nc, "s_wp3")]
    s_kv = [Sem(nc, "s_kv0"), Sem(nc, "s_kv1")]
    s_y = [Sem(nc, "s_y0"), Sem(nc, "s_y1")]
    s_wq = Sem(nc, "s_wq")
    s_sk = [Sem(nc, "s_sk0"), Sem(nc, "s_sk1")]
    s_sv = [Sem(nc, "s_sv0"), Sem(nc, "s_sv1")]
    s_wt = Sem(nc, "s_wt")
    s_wr2 = Sem(nc, "s_wr2")
    s_wt2 = Sem(nc, "s_wt2")
    s_pr = Sem(nc, "s_pr")
    s_bulk = Sem(nc, "s_bulk")
    s_flg = Sem(nc, "s_flg")

    sy, gp, te = nc.sync, nc.gpsimd, nc.tensor
    s_t = Sem(nc, "s_t")
    s_v = Sem(nc, "s_v")
    s_sc = Sem(nc, "s_sc")
    ve = AutoChain(nc.vector, s_v)
    sc = AutoChain(nc.scalar, s_sc)

    def W(eng, sem, val):
        if val > 0:
            eng.wait_ge(sem.h, val)

    ALL_SEMS = [s_c, s_x[0], s_x[1], s_wr, s_br, s_wf[0], s_wf[1], s_wf[2],
                s_wp[0], s_wp[1], s_wp[2], s_wp[3],
                s_kv[0], s_kv[1], s_y[0], s_y[1], s_wq, s_sk[0],
                s_sk[1], s_sv[0], s_sv[1], s_wt, s_wr2, s_wt2, s_pr, s_t,
                s_v, s_sc]

    def phase_join():
        for e in (sy, gp, te, nc.vector, nc.scalar):
            for sem in ALL_SEMS:
                W(e, sem, sem.n)

    ident_sb = nc.alloc_sbuf_tensor("ident_sb", [P, P], BF16)
    ones_sb = nc.alloc_sbuf_tensor("ones_sb", [1, P], F32)
    qT_sb = nc.alloc_sbuf_tensor("qT_sb", [P, 8 * TCORE], BF16)
    attn_sb = nc.alloc_sbuf_tensor("attn_sb", [P, TPC * D], BF16)
    ffn_sb = nc.alloc_sbuf_tensor("ffn_sb", [P, TPC * D], BF16)
    mr_sb = nc.alloc_sbuf_tensor("mr_sb", [P, 2 * TPC], F32)
    scr = {}
    flgsb = nc.alloc_sbuf_tensor("flgsb", [1, 8], F32)
    s_am = nc.alloc_sbuf_tensor("scr_am", [P, HID // 32], F32)
    s_mm = nc.alloc_sbuf_tensor("scr_m", [P, HID // 32], F32)
    s_s1 = nc.alloc_sbuf_tensor("scr_s1", [P, 32], F32)
    expb = nc.alloc_sbuf_tensor("expb", [P, 1], F32)

    ve.memset(expb[:], EXP_SHIFT)
    c_expb = s_v.n
    ld = sy.dma_start(ident_sb[:], ident_d)
    s_c.inc(ld, 16)
    ld = sy.dma_start(ones_sb[:], ones_d)
    s_c.inc(ld, 16)
    c_consts = s_c.n

    def quant_free(out_bf, in_f32, width):
        nb = width // 32
        x3 = in_f32.rearrange("p (b k) -> p b k", k=32)
        ve.tensor_reduce(s_am[:, 0:nb], x3, AX.X, ALU.max,
                         apply_absolute_value=True)
        ve.tensor_scalar(s_mm[:, 0:nb].bitcast(U32), s_am[:, 0:nb].bitcast(U32),
                         EXPO_MASK, None, ALU.bitwise_and)
        ve.tensor_scalar_mul(s_mm[:, 0:nb], s_mm[:, 0:nb], MAGIC)
        mb = s_mm[:, 0:nb].unsqueeze(2).broadcast_to((P, nb, 32))
        t3 = scr["a"][:, 0:width].rearrange("p (b k) -> p b k", k=32)
        o3 = out_bf.rearrange("p (b k) -> p b k", k=32)
        ve.tensor_tensor(t3, x3, mb, ALU.add)
        ve.tensor_tensor(o3, t3, mb, ALU.subtract)

    def stats_acc(tile_f32, sem_load, c_load, tt):
        W(sc, sem_load, c_load)
        sc.activation(scr["b"][:, 0:D], tile_f32, AF.Copy,
                      accum_out=s_s1[:, tt:tt + 1])
        sc.activation(scr["b"][:, 0:D], tile_f32, AF.Square,
                      accum_out=s_s1[:, 4 + tt:5 + tt])
        return s_sc.n

    def stats_finalize(nt, slot0_mr):
        ve.tensor_scalar_mul(s_s1[:, 8:8 + nt], s_s1[:, 0:nt], 1.0 / D)
        ve.tensor_scalar_mul(s_s1[:, 16:16 + nt], s_s1[:, 4:4 + nt], 1.0 / D)
        ve.tensor_tensor(s_s1[:, 20:20 + nt], s_s1[:, 8:8 + nt],
                         s_s1[:, 8:8 + nt], ALU.mult)
        ve.tensor_tensor(s_s1[:, 20:20 + nt], s_s1[:, 16:16 + nt],
                         s_s1[:, 20:20 + nt], ALU.subtract)
        ve.tensor_scalar_add(s_s1[:, 20:20 + nt], s_s1[:, 20:20 + nt], 1e-5)
        c_var = s_v.n
        W(sc, s_v, c_var)
        sc.sqrt(s_s1[:, 24:24 + nt], s_s1[:, 20:20 + nt])
        c_sd = s_sc.n
        W(ve, s_sc, c_sd)
        ve.reciprocal(s_s1[:, 12:12 + nt], s_s1[:, 24:24 + nt])
        if slot0_mr is not None:
            ve.tensor_copy(mr_sb[:].rearrange("p (s two) -> p s two", two=2)
                           [:, slot0_mr:slot0_mr + nt, 0:1],
                           s_s1[:, 8:8 + nt].unsqueeze(2))
            ve.tensor_copy(mr_sb[:].rearrange("p (s two) -> p s two", two=2)
                           [:, slot0_mr:slot0_mr + nt, 1:2],
                           s_s1[:, 12:12 + nt].unsqueeze(2))
        return s_v.n

    def stats_norm(tile_f32, sem_load, c_load, slot_mr, xn_out, prev_sb_reader_v):
        W(sc, sem_load, c_load)
        W(sc, s_v, prev_sb_reader_v)
        sc.square(scr["b"][:, 0:D], tile_f32)
        c_sq = s_sc.n
        W(ve, sem_load, c_load)
        ve.reduce_sum(s_s1[:, 0:1], tile_f32, AX.X)
        W(ve, s_sc, c_sq)
        ve.reduce_sum(s_s1[:, 1:2], scr["b"][:, 0:D], AX.X)
        ve.tensor_scalar_mul(s_s1[:, 2:3], s_s1[:, 0:1], 1.0 / D)
        ve.tensor_scalar_mul(s_s1[:, 3:4], s_s1[:, 1:2], 1.0 / D)
        ve.tensor_tensor(s_s1[:, 4:5], s_s1[:, 2:3], s_s1[:, 2:3], ALU.mult)
        ve.tensor_tensor(s_s1[:, 5:6], s_s1[:, 3:4], s_s1[:, 4:5], ALU.subtract)
        ve.tensor_scalar_add(s_s1[:, 5:6], s_s1[:, 5:6], 1e-5)
        c_var = s_v.n
        W(sc, s_v, c_var)
        sc.sqrt(s_s1[:, 6:7], s_s1[:, 5:6])
        c_sd = s_sc.n
        W(ve, s_sc, c_sd)
        ve.reciprocal(s_s1[:, 7:8], s_s1[:, 6:7])
        if slot_mr is not None:
            ve.tensor_copy(mr_sb[:, 2 * slot_mr:2 * slot_mr + 1], s_s1[:, 2:3])
            ve.tensor_copy(mr_sb[:, 2 * slot_mr + 1:2 * slot_mr + 2], s_s1[:, 7:8])
        ve.tensor_scalar(xn_out, tile_f32, s_s1[:, 2:3],
                         s_s1[:, 7:8], ALU.subtract, ALU.mult)
        return s_v.n, c_sq

    with nc.sbuf_tensor("wq_sb", [P, 8 * E3], BF16) as wq_sb:
        for dc in range(8):
            ins = gp.dma_start(wq_sb[:, dc * E3:(dc + 1) * E3],
                               w_qkv[dc * P:(dc + 1) * P, :])
            c_wq = s_wq.inc(ins, 16)

        with ExitStack() as _wst:
            _e = _wst.enter_context
            WMAX = 2048
            wraw = [_e(nc.sbuf_tensor(f"wraw{i}", [P, WMAX], F32))
                    for i in range(2)]
            scr["a"] = _e(nc.sbuf_tensor("scrWa", [P, D], F32))
            wqbf = [_e(nc.sbuf_tensor(f"wqbf{i}", [P, WMAX], BF16))
                    for i in range(2)]
            wtr = [_e(nc.sbuf_tensor(f"wtr{i}", [P, WMAX], BF16))
                   for i in range(2)]
            pw = [_e(nc.psum_tensor(f"pw{i}", [P, P], BF16)) for i in range(2)]
            tr_rel = [0, 0]
            s_wld = [s_wr, s_wr2]
            s_wst = [s_wt, s_wt2]
            c_store = [0, 0]
            c_vq = [0, 0]
            c_tr = [0, 0]
            wg = [0]

            def w_quant_transpose(src, rows, width, dst):
                dst3 = dst.rearrange("(c p) n -> p c n", p=P)
                for ch in range(rows // P):
                    for wo in range(0, width, WMAX):
                        pw_ = min(WMAX, width - wo)
                        p = wg[0] % 2
                        wg[0] += 1
                        W(sy, s_v, c_vq[p])
                        ins = sy.dma_start(
                            wraw[p][:, 0:pw_],
                            src[ch * P:(ch + 1) * P, wo:wo + pw_])
                        c_load = s_wld[p].inc(ins, 16)
                        W(ve, s_wld[p], c_load)
                        W(ve, s_t, c_tr[p])
                        for qo in range(0, pw_, D):
                            qw = min(D, pw_ - qo)
                            quant_free(wqbf[p][:, qo:qo + qw],
                                       wraw[p][:, qo:qo + qw], qw)
                        c_vq[p] = s_v.n
                        W(sc, s_wst[p], c_store[p])
                        for j in range(pw_ // P):
                            W(te, s_v, c_vq[p])
                            W(te, s_sc, tr_rel[j % 2])
                            c_tp = s_t.inc(te.transpose(
                                pw[j % 2][:], wqbf[p][:, j * P:(j + 1) * P],
                                ident_sb[:]))
                            W(sc, s_t, c_tp)
                            sc.activation(wtr[p][:, j * P:(j + 1) * P],
                                          pw[j % 2][:], AF.Copy)
                            tr_rel[j % 2] = s_sc.n
                        c_tr[p] = s_t.n
                        W(gp, s_sc, tr_rel[(pw_ // P - 1) % 2])
                        W(gp, s_sc, tr_rel[(pw_ // P) % 2])
                        ins = gp.dma_start(
                            dst3[:, wo // P:(wo + pw_) // P,
                                 ch * P:(ch + 1) * P],
                            wtr[p][:, 0:pw_].rearrange("p (c q) -> p c q", q=P))
                        c_store[p] = s_wst[p].inc(ins, 16)

            te.wait_ge(s_c.h, c_consts)
            w_quant_transpose(w_fc, HID, D, wfcqT_d)
            w_quant_transpose(w_proj, D, HID, wprojqT_d)
        c_wprep = (s_wt.n, s_wt2.n)

        phase_join()
        with ExitStack() as _kst:
            _e = _kst.enter_context
            xf = [_e(nc.sbuf_tensor(f"xf{i}", [P, D], F32)) for i in range(4)]
            scr["b"] = _e(nc.sbuf_tensor("scrKb", [P, D], F32))
            xn = [_e(nc.sbuf_tensor(f"xn{i}", [P, D], BF16)) for i in range(4)]
            xT = [_e(nc.sbuf_tensor(f"xT{i}", [P, 8 * 512], BF16)) for i in range(2)]
            kst = [_e(nc.sbuf_tensor(f"kst{i}", [P, 512], BF16)) for i in range(2)]
            vst = [_e(nc.sbuf_tensor(f"vst{i}", [P, H * 65], BF16)) for i in range(2)]
            pt = [_e(nc.psum_tensor(f"pt{i}", [P, 8 * P], BF16)) for i in range(2)]
            pk = [_e(nc.psum_tensor(f"pk{i}", [P, 512], F32)) for i in range(2)]
            pv = [_e(nc.psum_tensor(f"pv{i}", [P, 512], F32)) for i in range(2)]

            ve.memset(
                vst[0][:].rearrange("p (h c) -> p h c", c=65)[:, :, 64:65], 1.0)
            ve.memset(
                vst[1][:].rearrange("p (h c) -> p h c", c=65)[:, :, 64:65], 1.0)
            c_vones = s_v.n

            rel_xf_sc = [0, 0]
            rel_xf_v = [0, 0]
            rel_xn = [0, 0, 0, 0]
            rel_xT = [0, 0]
            rel_kst = [0, 0]
            rel_vst = [c_vones, c_vones]
            rel_vst_gd = [0, 0]
            tr_rel = [0, 0]
            kev = [0, 0]
            vev = [0, 0]
            prev_sb_reader = 0

            pid_rv = nc.partition_id([mybir.EngineType.Pool])
            rb = gp.scalar_reg_alu(ALU.mod, pid_rv, 2)
            rkbase = gp.scalar_reg_alu(ALU.mult, rb, D)
            rvbase = gp.scalar_reg_alu(ALU.mult, rb, S)
            ve.memset(flgsb[:], 0.0)

            for ch in range(NT * P // 512):
                tloc = ch
                cpar = ch % 2
                for tt in range(4):
                    i = ch * 4 + tt
                    if tt == 0:
                        W(sy, s_sc, rel_xf_sc[0])
                        W(sy, s_v, rel_xf_v[0])
                    ins = sy.dma_start(xf[tt][:], x_kv[i * P:(i + 1) * P, :])
                    c_load4 = s_x[cpar].inc(ins, 16)
                for tt in range(4):
                    stats_acc(xf[tt][:], s_x[cpar], c_load4, tt)
                c_acc = s_sc.n
                W(ve, s_sc, c_acc)
                stats_finalize(4, None)
                rel_xf_sc[0] = s_sc.n
                for tt in range(4):
                    W(ve, s_t, rel_xn[tt])
                    ve.tensor_scalar(xn[tt][:], xf[tt][:],
                                     s_s1[:, 8 + tt:9 + tt],
                                     s_s1[:, 12 + tt:13 + tt],
                                     ALU.subtract, ALU.mult)
                c_norm = s_v.n
                rel_xf_v[0] = s_v.n
                for tt in range(4):
                    bk = tt % 2
                    W(te, s_v, c_norm)
                    W(te, s_v, tr_rel[bk])
                    W(te, s_t, rel_xT[cpar])
                    for dc in range(8):
                        c_tp = s_t.inc(te.transpose(pt[bk][:, dc * P:(dc + 1) * P],
                                                    xn[tt][:, dc * P:(dc + 1) * P],
                                                    ident_sb[:]))
                    rel_xn[tt] = s_t.n
                    W(ve, s_t, c_tp)
                    ve.tensor_copy(
                        xT[cpar][:].rearrange("p (c f) -> p c f", f=512)
                        [:, :, tt * P:(tt + 1) * P],
                        pt[bk][:].rearrange("p (c q) -> p c q", q=P))
                    tr_rel[bk] = s_v.n
                c_xT = s_v.n

                W(te, s_wq, c_wq)
                for ec in range(8):
                    kp = ec % 2
                    W(te, s_v, c_xT)
                    W(te, s_v, kev[kp])
                    for dc in range(8):
                        mm = te.matmul(
                            pk[kp][:],
                            wq_sb[:, dc * E3 + D + ec * P: dc * E3 + D + (ec + 1) * P],
                            xT[cpar][:, dc * 512:(dc + 1) * 512],
                            start=(dc == 0), stop=(dc == 7))
                    c_kmm = s_t.inc(mm)
                    W(ve, s_t, c_kmm)
                    W(ve, s_sk[kp], rel_kst[kp])
                    ve.tensor_copy(kst[kp][:], pk[kp][:])
                    kev[kp] = s_v.n
                    W(gp, s_v, kev[kp])
                    ins = gp.dma_start(
                        kT_loc[ec * P:(ec + 1) * P, tloc * 512:(tloc + 1) * 512],
                        kst[kp][:])
                    rel_kst[kp] = s_sk[kp].inc(ins, 16)

                for tt in range(4):
                    vp = tt % 2
                    for nh in range(2):
                        W(te, s_v, c_xT)
                        W(te, s_v, vev[vp])
                        for dc in range(8):
                            mm = te.matmul(
                                pv[vp][:],
                                xT[cpar][:, dc * 512 + tt * P: dc * 512 + (tt + 1) * P],
                                wq_sb[:, dc * E3 + 2 * D + nh * 512:
                                      dc * E3 + 2 * D + (nh + 1) * 512],
                                start=(dc == 0), stop=(dc == 7))
                        c_vmm = s_t.inc(mm)
                        W(ve, s_t, c_vmm)
                        W(ve, s_sv[vp], rel_vst_gd[vp])
                        ve.tensor_copy(
                            vst[vp][:].rearrange("p (h c) -> p h c", c=65)
                            [:, nh * 8:(nh + 1) * 8, 0:64],
                            pv[vp][:].rearrange("p (h e) -> p h e", e=64))
                        vev[vp] = s_v.n
                    W(gp, s_v, vev[vp])
                    ins = gp.dma_start(
                        v_loc[(tloc * 4 + tt) * P:(tloc * 4 + tt + 1) * P, :],
                        vst[vp][:])
                    rel_vst_gd[vp] = s_sv[vp].inc(ins, 16)
                rel_xT[cpar] = s_t.n

        c_kvdone = (s_sk[0].n, s_sk[1].n, s_sv[0].n, s_sv[1].n)
        W(gp, s_sk[0], c_kvdone[0])
        W(gp, s_sk[1], c_kvdone[1])
        W(gp, s_sv[0], c_kvdone[2])
        W(gp, s_sv[1], c_kvdone[3])
        ins = gp.dma_start(kT_sh[bass.ds(rkbase, D)], kT_loc)
        s_bulk.inc(ins, 16)
        ins = gp.dma_start(v_sh[bass.ds(rvbase, S)], v_loc)
        c_bulk = s_bulk.inc(ins, 16)
        gp.wait_ge(s_bulk.h, c_bulk)
        gp.wait_ge(s_c.h, c_consts)
        s_flg.inc(gp.dma_start(kvflag_sh[bass.ds(rb, 1), :],
                               ones_sb[:, 0:8]), 16)
        phase_join()
        with ExitStack() as _qst:
            _e = _qst.enter_context
            qf = [_e(nc.sbuf_tensor(f"qf{i}", [P, D], F32)) for i in range(4)]
            scr["b"] = _e(nc.sbuf_tensor("scrQb", [P, D], F32))
            qn = [_e(nc.sbuf_tensor(f"qn{i}", [P, D], BF16)) for i in range(4)]
            qTb = [_e(nc.sbuf_tensor(f"qTb{i}", [P, 8 * 512], BF16)) for i in range(2)]
            qt = [_e(nc.psum_tensor(f"qt{i}", [P, 8 * P], BF16)) for i in range(2)]
            pq = [_e(nc.psum_tensor(f"pq{i}", [P, 512], F32)) for i in range(2)]
            rel_qf_sc = [s_sc.n, s_sc.n]
            rel_qf_v = [s_v.n, s_v.n]
            rel_qn = [s_t.n] * 4
            rel_qT = [s_t.n, s_t.n]
            tr_rel = [s_v.n, s_v.n]
            qev = [0, 0]
            prev_sb_reader = s_v.n
            for chq in range(4):
                cpar = chq % 2
                for tt in range(4):
                    slot = chq * 4 + tt
                    if tt == 0:
                        W(sy, s_sc, rel_qf_sc[0])
                        W(sy, s_v, rel_qf_v[0])
                    ins = sy.dma_start(qf[tt][:], x_q[slot * P:(slot + 1) * P, :])
                    c_load4 = s_x[cpar].inc(ins, 16)
                for tt in range(4):
                    stats_acc(qf[tt][:], s_x[cpar], c_load4, tt)
                c_acc = s_sc.n
                W(ve, s_sc, c_acc)
                stats_finalize(4, chq * 4)
                rel_qf_sc[0] = s_sc.n
                for tt in range(4):
                    W(ve, s_t, rel_qn[tt])
                    ve.tensor_scalar(qn[tt][:], qf[tt][:],
                                     s_s1[:, 8 + tt:9 + tt],
                                     s_s1[:, 12 + tt:13 + tt],
                                     ALU.subtract, ALU.mult)
                c_norm = s_v.n
                rel_qf_v[0] = s_v.n
                for tt in range(4):
                    bk = tt % 2
                    W(te, s_v, c_norm)
                    W(te, s_v, tr_rel[bk])
                    W(te, s_t, rel_qT[cpar])
                    for dc in range(8):
                        c_tp = s_t.inc(te.transpose(qt[bk][:, dc * P:(dc + 1) * P],
                                                    qn[tt][:, dc * P:(dc + 1) * P],
                                                    ident_sb[:]))
                    rel_qn[tt] = s_t.n
                    W(ve, s_t, c_tp)
                    ve.tensor_copy(
                        qTb[cpar][:].rearrange("p (c f) -> p c f", f=512)
                        [:, :, tt * P:(tt + 1) * P],
                        qt[bk][:].rearrange("p (c q) -> p c q", q=P))
                    tr_rel[bk] = s_v.n
                c_qT = s_v.n
                for ec in range(8):
                    qp = ec % 2
                    W(te, s_v, c_qT)
                    W(te, s_v, qev[qp])
                    for dc in range(8):
                        mm = te.matmul(
                            pq[qp][:],
                            wq_sb[:, dc * E3 + ec * P: dc * E3 + (ec + 1) * P],
                            qTb[cpar][:, dc * 512:(dc + 1) * 512],
                            start=(dc == 0), stop=(dc == 7))
                    c_qmm = s_t.inc(mm)
                    W(ve, s_t, c_qmm)
                    ve.tensor_scalar(
                        qT_sb[:, ec * TCORE + chq * 512: ec * TCORE + (chq + 1) * 512],
                        pq[qp][:], 0.125, None, ALU.mult)
                    qev[qp] = s_v.n
                rel_qT[cpar] = s_t.n

    phase_join()
    with ExitStack() as _fst:
        _e = _fst.enter_context
        bfc_hl = _e(nc.sbuf_tensor("bfc_hl", [2, HID], BF16))
        bpr_hl = _e(nc.sbuf_tensor("bpr_hl", [2, D], BF16))
        ones2 = _e(nc.sbuf_tensor("ones2", [2, P], BF16))
        bfrow = _e(nc.sbuf_tensor("bfrow", [1, D], F32))
        blo = _e(nc.sbuf_tensor("blo", [1, D], BF16))
        fx = [_e(nc.sbuf_tensor(f"fx{i}", [P, D], F32)) for i in range(2)]
        fxn = _e(nc.sbuf_tensor("fxn", [P, D], F32))
        scr["a"] = _e(nc.sbuf_tensor("scrFa", [P, D], F32))
        fxq = [_e(nc.sbuf_tensor(f"fxq{i}", [P, D], BF16)) for i in range(2)]
        xqT = _e(nc.sbuf_tensor("xqT", [P, 2 * D], BF16))
        wfcb = [_e(nc.sbuf_tensor(f"wfcb{i}", [P, 8 * 512], BF16)) for i in range(3)]
        hgl = [_e(nc.sbuf_tensor(f"hgl{i}", [P, 1024], F32)) for i in range(2)]
        hqb = [_e(nc.sbuf_tensor(f"hqb{i}", [P, 1024], BF16)) for i in range(2)]
        hqT = _e(nc.sbuf_tensor("hqT", [P, 32 * 256], BF16))
        wprb = [_e(nc.sbuf_tensor(f"wprb{i}", [P, D], BF16)) for i in range(4)]

        ve.memset(ones2[:], 1.0)
        c_b_gate = 0
        c_lo = 0
        for bsrc, hl, width in ((b_fc, bfc_hl, HID), (b_proj, bpr_hl, D)):
            for qo in range(0, width, D):
                W(sy, s_v, c_b_gate)
                ins = sy.dma_start(bfrow[:], bsrc[:, qo:qo + D])
                c_br = s_br.inc(ins, 16)
                W(ve, s_br, c_br)
                W(ve, s_wq, c_lo)
                ve.tensor_copy(hl[0:1, qo:qo + D], bfrow[:])
                ve.tensor_copy(scr["a"][0:1, :], hl[0:1, qo:qo + D])
                ve.tensor_tensor(blo[:], bfrow[:], scr["a"][0:1, :],
                                 ALU.subtract)
                c_b_gate = s_v.n
                W(gp, s_v, c_b_gate)
                ins = gp.dma_start(hl[1:2, qo:qo + D], blo[:])
                c_lo = s_wq.inc(ins, 16)
        c_bias_ready = s_v.n

        with ExitStack() as _fps:
            _e2 = _fps.enter_context
            pf = [_e2(nc.psum_tensor(f"pf{i}", [P, 1024], F32)) for i in range(2)]
            pp2 = _e2(nc.psum_tensor("pp2", [P, 2048], F32))
            trb = [pp2[:, 0:512].bitcast(BF16), pp2[:, 512:1024].bitcast(BF16)]
            tr_rel = [s_v.n, s_v.n]
            rel_fx_v = [s_v.n, s_v.n]
            rel_fxq = [s_t.n, s_t.n]
            fev = [0, 0]
            rel_wfc = [s_t.n] * 3
            rel_wpr = [s_t.n] * 4
            c_wfc_ld = [0, 0, 0]
            c_wpr_ld = [0, 0, 0, 0]
            rel_hgl_v = [0, 0]
            c_gelu = [0, 0]
            rel_hqb_t = [s_t.n, s_t.n]
            rel_xqT = s_t.n
            rel_hqT = s_t.n
            c_pp_ev = 0

            def xq_prep(sc0):
                cqs = []
                for tt in range(2):
                    slot = sc0 * 2 + tt
                    par = tt % 2
                    W(sy, s_v, rel_fx_v[par])
                    ins = sy.dma_start(fx[par][:],
                                       x_q[slot * P:(slot + 1) * P, :])
                    c_load = s_x[par].inc(ins, 16)
                    W(ve, s_x[par], c_load)
                    ve.tensor_scalar(fxn[:], fx[par][:],
                                     mr_sb[:, 2 * slot:2 * slot + 1],
                                     mr_sb[:, 2 * slot + 1:2 * slot + 2],
                                     ALU.subtract, ALU.mult)
                    rel_fx_v[par] = s_v.n
                    W(ve, s_t, rel_fxq[par])
                    quant_free(fxq[par][:], fxn[:], D)
                    cqs.append(s_v.n)
                return cqs

            cqs = xq_prep(0)
            for sc0 in range(8):
                for tt in range(2):
                    par = tt % 2
                    bk = tt % 2
                    W(te, s_v, cqs[tt])
                    W(te, s_v, tr_rel[bk])
                    W(te, s_v, c_pp_ev)
                    W(te, s_t, rel_xqT)
                    for dc in range(8):
                        c_tp = s_t.inc(te.transpose(trb[bk][:, dc * P:(dc + 1) * P],
                                                    fxq[par][:, dc * P:(dc + 1) * P],
                                                    ident_sb[:]))
                    W(ve, s_t, c_tp)
                    ve.tensor_copy(xqT[:, tt * D:(tt + 1) * D], trb[bk][:])
                    tr_rel[bk] = s_v.n
                    rel_fxq[par] = s_t.n
                c_xqT = s_v.n

                def emit_hq_transposes(pn, pfp, c_hq_p):
                    bk3 = pn % 2
                    W(te, s_v, c_hq_p)
                    W(te, s_v, tr_rel[bk3])
                    c_tp = None
                    for j in range(8):
                        c_tp = s_t.inc(te.transpose(
                            trb[bk3][:, j * P:(j + 1) * P],
                            hqb[pfp][:, j * P:(j + 1) * P],
                            ident_sb[:]))
                    rel_hqb_t[pfp] = s_t.n
                    W(ve, s_t, c_tp)
                    W(ve, s_t, rel_hqT)
                    for tt in range(2):
                        ve.tensor_copy(
                            hqT[:].rearrange("p (c f) -> p c f", f=256)
                            [:, pn * 4:(pn + 1) * 4, tt * P:(tt + 1) * P],
                            trb[bk3][:, tt * 512:(tt + 1) * 512]
                            .rearrange("p (c q) -> p c q", q=P))
                    tr_rel[bk3] = s_v.n

                pending = None
                for nch in range(8):
                    wp = nch % 3
                    fp_ = nch % 2
                    W(gp, s_t, rel_wfc[wp])
                    ins = gp.dma_start(
                        wfcb[wp][:].rearrange("p (c n) -> p c n", n=512),
                        wfcqT_d.rearrange("(c p) n -> p c n", p=P)
                        [:, :, nch * 512:(nch + 1) * 512])
                    c_wfc_ld[wp] = s_wf[wp].inc(ins, 16)
                    if nch == 0:
                        W(te, s_v, c_xqT)
                    W(te, s_wf[wp], c_wfc_ld[wp])
                    W(te, s_sc, fev[fp_])
                    if sc0 == 0 and nch == 0:
                        W(te, s_v, c_bias_ready)
                        W(te, s_wq, c_lo)
                    for tt in range(2):
                        for dc in range(8):
                            te.matmul(
                                pf[fp_][:, tt * 512:(tt + 1) * 512],
                                xqT[:, tt * D + dc * P: tt * D + (dc + 1) * P],
                                wfcb[wp][:, dc * 512:(dc + 1) * 512],
                                start=(dc == 0), stop=False)
                        mm = te.matmul(
                            pf[fp_][:, tt * 512:(tt + 1) * 512],
                            ones2[:], bfc_hl[:, nch * 512:(nch + 1) * 512],
                            start=False, stop=True)
                    c_fmm = s_t.inc(mm)
                    rel_wfc[wp] = s_t.n
                    if pending is not None:
                        emit_hq_transposes(*pending)
                    W(sc, s_t, c_fmm)
                    W(sc, s_v, rel_hgl_v[fp_])
                    sc.activation(hgl[fp_][:], pf[fp_][:], AF.Gelu)
                    c_gelu[fp_] = s_sc.n
                    fev[fp_] = c_gelu[fp_]
                    W(ve, s_sc, c_gelu[fp_])
                    W(ve, s_t, rel_hqb_t[fp_])
                    quant_free(hqb[fp_][:], hgl[fp_][:], 1024)
                    c_hq = s_v.n
                    rel_hgl_v[fp_] = c_hq
                    pending = (nch, fp_, c_hq)
                emit_hq_transposes(*pending)
                c_hqT = s_v.n
                if sc0 + 1 < 8:
                    cqs = xq_prep(sc0 + 1)

                for ncr in range(32):
                    wp = ncr % 4
                    W(gp, s_t, rel_wpr[wp])
                    ins = gp.dma_start(wprb[wp][:],
                                       wprojqT_d[ncr * P:(ncr + 1) * P, :])
                    c_wpr_ld[wp] = s_wp[wp].inc(ins, 16)
                    W(te, s_wp[wp], c_wpr_ld[wp])
                    if ncr == 0:
                        W(te, s_v, c_hqT)
                        W(te, s_v, tr_rel[0])
                        W(te, s_v, tr_rel[1])
                        for tt in range(2):
                            for mh in range(2):
                                te.matmul(
                                    pp2[:, tt * 1024 + mh * 512:
                                        tt * 1024 + (mh + 1) * 512],
                                    ones2[:], bpr_hl[:, mh * 512:(mh + 1) * 512],
                                    start=True, stop=False,
                                    skip_group_check=True)
                    for tt in range(2):
                        for mh in range(2):
                            mm = te.matmul(
                                pp2[:, tt * 1024 + mh * 512:
                                    tt * 1024 + (mh + 1) * 512],
                                hqT[:, ncr * 256 + tt * P:
                                    ncr * 256 + (tt + 1) * P],
                                wprb[wp][:, mh * 512:(mh + 1) * 512],
                                start=False, stop=(ncr == 31),
                                skip_group_check=True)
                    rel_wpr[wp] = s_t.inc(mm)
                c_proj = s_t.n
                W(ve, s_t, c_proj)
                for rr in range(4):
                    ve.tensor_copy(
                        ffn_sb[:, sc0 * 2 * D + rr * 512:
                               sc0 * 2 * D + (rr + 1) * 512],
                        pp2[:, rr * 512:(rr + 1) * 512])
                c_pp_ev = s_v.n
                rel_xqT = s_t.n
                rel_hqT = s_t.n

    rpn = gp.scalar_reg_alu(ALU.add, pid_rv, 1)
    rr_ = gp.scalar_reg_alu(ALU.mod, rpn, 2)
    if PAIR_POLL:
        for _ in range(32):
            s_flg.inc(gp.dma_start(flgsb[:], kvflag_sh[bass.ds(rr_, 1), :]),
                      16)
        rf = gp.alloc_register()
        gp.reg_load(rf, flgsb[0:1, 0:1].bitcast(U32))
        with gp.While(rf == 0):
            gp.reg_load(rf, flgsb[0:1, 0:1].bitcast(U32))
        gp.sem_inc(s_pr.h, 1)
    else:
        ins = gp.dma_start(flgsb[:], kvflag_sh[bass.ds(rr_, 1), :])
        c_fl = s_flg.inc(ins, 16)
        gp.wait_ge(s_flg.h, c_fl)
        gp.sem_inc(s_pr.h, 1)

    phase_join()
    with ExitStack() as _ast:
        _e = _ast.enter_context
        kbuf = [_e(nc.sbuf_tensor(f"kbuf{i}", [P, S], BF16)) for i in range(2)]
        kraw = _e(nc.sbuf_tensor("kraw", [P, S], BF16))
        masks_sb = _e(nc.sbuf_tensor("masks_sb", [P, 8 * P], BF16))
        ld = sy.dma_start(masks_sb[:], masks)
        c_masks = s_c.inc(ld, 16)
        vbuf = [_e(nc.sbuf_tensor(f"vbuf{i}", [P, NT * 130], BF16)) for i in range(2)]
        pT = [_e(nc.sbuf_tensor(f"pTb{i}", [P, 2048], BF16)) for i in range(5)]
        pTd = [_e(nc.sbuf_tensor(f"pTd{i}", [P, 256], BF16)) for i in range(6)]
        sps = _e(nc.psum_tensor("sps", [P, 2048], F32))
        avp = _e(nc.psum_tensor("avp", [P, 2048], F32))

        def av_off(sh):
            return 128 * sh

        c_kvload = [0, 0]
        rel_kv = [0, 0]
        c_avev = 0
        c_exp = {}
        c_mask = {}
        c_avt = {}
        dg_rel0 = 0

        def issue_kv_load(bhp, rel_kraw_t):
            b, hp = bhp // 8, bhp % 8
            par = bhp % 2
            W(sy, s_t, rel_kv[par])
            W(sy, s_t, rel_kraw_t)
            W(sy, s_sk[0], c_kvdone[0])
            W(sy, s_sk[1], c_kvdone[1])
            W(sy, s_sv[0], c_kvdone[2])
            W(sy, s_sv[1], c_kvdone[3])
            kview = kT_sh[b * D:(b + 1) * D].rearrange("c (j dd) -> j c dd",
                                                       dd=64)
            for cg in range(8):
                ins = sy.dma_start(
                    kraw[:].rearrange("j (c dd) -> j c dd", dd=64)
                    [:, cg * 16:(cg + 1) * 16, :],
                    kview[:, hp * P + cg * 16: hp * P + (cg + 1) * 16, :])
                s_kv[par].inc(ins, 16)
            c = 0
            for g in range(8):
                ins = sy.dma_start(
                    vbuf[par][:, g * 1040:(g + 1) * 1040]
                    .rearrange("p (k c) -> p k c", c=130),
                    v_sh[b * S:(b + 1) * S]
                    .rearrange("(g k p) c -> g p k c", p=P, k=8)
                    [g, :, :, 2 * hp * 65: 2 * hp * 65 + 130])
                c = s_kv[par].inc(ins, 16)
            c_kvload[par] = c

        ve.wait_ge(s_c.h, c_masks)
        sy.wait_ge(s_pr.h, 1)
        sy.wait_ge(s_bulk.h, c_bulk)
        rel_kraw = 0
        c_exp_last = 0
        issue_kv_load(0, 0)
        for bhp in range(16):
            b, hp = bhp // 8, bhp % 8
            par = bhp % 2

            W(te, s_kv[par], c_kvload[par])
            W(te, s_sc, c_exp_last)
            trb = [sps[:, 0:512].bitcast(BF16), sps[:, 1024:1536].bitcast(BF16)]
            c_trev = {-2: 0, -1: 0}
            for g in range(16):
                bk = g % 2
                W(te, s_v, c_trev[g - 2])
                for i in range(4):
                    kt = g * 4 + i
                    for hl in range(2):
                        mm = te.transpose(
                            trb[bk][hl * 64:(hl + 1) * 64, i * P:(i + 1) * P],
                            kraw[:, (hl * 64 + kt) * 64:(hl * 64 + kt + 1) * 64],
                            ident_sb[:])
                c_tr = s_t.inc(mm)
                W(ve, s_t, c_tr)
                ve.tensor_copy(kbuf[par][:, g * 512:(g + 1) * 512],
                               trb[bk][:, 0:512])
                c_trev[g] = s_v.n
            rel_kraw = s_t.n
            c_kbuf = s_v.n

            if bhp + 1 < 16:
                issue_kv_load(bhp + 1, rel_kraw)
            c_exp.clear()
            c_mask.clear()
            c_avt.clear()
            for dd in range(1, 7):
                c_exp[-dd] = c_exp_last
                c_avt[-dd] = s_t.n
            md = [0]
            pTd_idx = {}
            dg_rel = [dg_rel0] * 6
            avp_started = set()
            qbase = hp * TCORE + b * 1024

            stages = []
            for kt in range(NT):
                c0 = (kt // 8) * P
                if c0 < 512:
                    stages.append((kt, c0, 512))
                    stages.append((kt, 512, 1024))
                else:
                    stages.append((kt, c0, 1024))
            nstg = len(stages)

            def _av_mm(si, sp, kt, hh, cl):
                sh = hh * 8 + cl
                off = av_off(sh)
                bk = sh // 4
                st = bk not in avp_started
                avp_started.add(bk)
                return te.matmul(
                    avp[:, off:off + 65],
                    pT[sp][:, hh * 1024 + cl * P: hh * 1024 + (cl + 1) * P],
                    vbuf[par][:, kt * 130 + hh * 65: kt * 130 + (hh + 1) * 65],
                    start=st, stop=(kt == 8 * (cl + 1) - 1),
                    skip_group_check=True)

            def emit_av_nd(si):
                kt, a0, a1 = stages[si]
                sp = si % 5
                start = kt // 8
                diag = a0 <= start * P < a1
                W(te, s_sc, c_exp[si])
                mm = None
                for hh in range(2):
                    for cl in range(max(start, a0 // P), a1 // P):
                        if diag and cl == start:
                            continue
                        mm = _av_mm(si, sp, kt, hh, cl)
                c_avt[si] = s_t.inc(mm) if mm is not None else s_t.n

            def emit_av_dg(si):
                kt, a0, a1 = stages[si]
                start = kt // 8
                if not (a0 <= start * P < a1):
                    return
                W(te, s_v, c_mask[si])
                pdx = pTd_idx[si]
                for hh in range(2):
                    sh = hh * 8 + start
                    off = av_off(sh)
                    bk = sh // 4
                    st = bk not in avp_started
                    avp_started.add(bk)
                    mm = te.matmul(
                        avp[:, off:off + 65],
                        pTd[pdx][:, hh * P:(hh + 1) * P],
                        vbuf[par][:, kt * 130 + hh * 65: kt * 130 + (hh + 1) * 65],
                        start=st, stop=(kt == 8 * (start + 1) - 1),
                        skip_group_check=True)
                dg_rel[pdx] = s_t.inc(mm)

            for si in range(nstg):
                kt, a0, a1 = stages[si]
                sp2 = si % 2
                sp = si % 5
                start = kt // 8
                lo = a0 % 512
                width = a1 - a0
                W(te, s_sc, c_exp[si - 2])
                if si == 0:
                    W(te, s_kv[par], c_kvload[par])
                    W(te, s_v, c_avev)
                    W(te, s_v, c_kbuf)
                for hh in range(2):
                    mm = te.matmul(
                        sps[:, hh * 1024 + sp2 * 512 + lo:
                            hh * 1024 + sp2 * 512 + lo + width],
                        kbuf[par][hh * 64:(hh + 1) * 64, kt * P:(kt + 1) * P],
                        qT_sb[hh * 64:(hh + 1) * 64, qbase + a0: qbase + a1],
                        start=True, stop=True)
                c_qk = s_t.inc(mm)
                if si >= 4:
                    emit_av_nd(si - 4)
                    emit_av_dg(si - 4)
                W(sc, s_t, c_qk)
                W(sc, s_t, c_avt[si - 5])
                src = sps[:].rearrange("p (h q) -> p h q", h=2)\
                    [:, :, sp2 * 512 + lo: sp2 * 512 + lo + width]
                dst = pT[sp][:].rearrange("p (h q) -> p h q", h=2)[:, :, a0:a1]
                sc.activation(dst, src, AF.Exp, bias=expb[:], scale=1.0)
                c_exp[si] = s_sc.n
                c_exp_last = c_exp[si]
                if a0 <= start * P < a1:
                    pdx = md[0] % 6
                    md[0] += 1
                    pTd_idx[si] = pdx
                    W(ve, s_sc, c_exp[si])
                    W(ve, s_t, dg_rel[pdx])
                    ve.tensor_tensor(
                        pTd[pdx][:].rearrange("p (h q) -> p h q", h=2),
                        pT[sp][:].rearrange("p (h q) -> p h q", h=2)
                        [:, :, start * P:(start + 1) * P],
                        masks_sb[:, (kt % 8) * P:(kt % 8 + 1) * P]
                        .unsqueeze(1).broadcast_to((P, 2, P)),
                        ALU.mult)
                    c_mask[si] = s_v.n

            for fsi in (nstg - 4, nstg - 3, nstg - 2, nstg - 1):
                emit_av_nd(fsi)
                emit_av_dg(fsi)
            c_av_last = s_t.n
            dg_rel0 = s_t.n
            rel_kv[par] = c_av_last

            W(ve, s_t, c_av_last)
            ve.reciprocal(
                s_s1[:, 0:16].unsqueeze(2),
                avp[:].rearrange("p (s c) -> p s c", c=128)[:, :, 64:65])
            dst = attn_sb[:].rearrange("p (s hh e) -> p s hh e", hh=16, e=64)\
                [:, b * 8:(b + 1) * 8, 2 * hp:2 * hp + 2, :]
            src = avp[:].rearrange("p (hh cl c) -> p cl hh c", hh=2, c=128)\
                [:, :, :, 0:64]
            rcp = s_s1[:, 0:16].rearrange("p (hh cl) -> p cl hh", hh=2)\
                .unsqueeze(3).broadcast_to((P, 8, 2, 64))
            ve.tensor_tensor(dst, src, rcp, ALU.mult)
            c_avev = s_v.n

    phase_join()
    with ExitStack() as _rst:
        _e = _rst.enter_context
        rx = [_e(nc.sbuf_tensor(f"rx{i}", [P, D], F32)) for i in range(2)]
        ro = [_e(nc.sbuf_tensor(f"ro{i}", [P, D], F32)) for i in range(2)]
        scr["a"] = _e(nc.sbuf_tensor("scrRa", [P, D], F32))
        scr["b"] = _e(nc.sbuf_tensor("scrRb", [P, D], F32))
        rel_rx = [s_v.n, s_v.n]
        rel_ro = [0, 0]
        for slot in range(TPC):
            par = slot % 2
            W(sy, s_v, rel_rx[par])
            ins = sy.dma_start(rx[par][:], x_q[slot * P:(slot + 1) * P, :])
            c_load = s_x[par].inc(ins, 16)
            W(ve, s_x[par], c_load)
            W(ve, s_y[par], rel_ro[par])
            ve.tensor_copy(scr["a"][:, 0:D], attn_sb[:, slot * D:(slot + 1) * D])
            ve.tensor_copy(scr["b"][:, 0:D], ffn_sb[:, slot * D:(slot + 1) * D])
            ve.tensor_tensor(ro[par][:], rx[par][:], scr["a"][:, 0:D], ALU.add)
            ve.tensor_tensor(ro[par][:], ro[par][:], scr["b"][:, 0:D], ALU.add)
            c_ro = s_v.n
            rel_rx[par] = c_ro
            W(sy, s_v, c_ro)
            ins = sy.dma_start(y[slot * P:(slot + 1) * P, :], ro[par][:])
            rel_ro[par] = s_y[par].inc(ins, 16)
        sy.wait_ge(s_y[0].h, s_y[0].n)
        sy.wait_ge(s_y[1].h, s_y[1].n)

    return nc


_CACHE = {}


def _masks_for_core(c):
    m = np.zeros((P, 8 * P), dtype=np.float32)
    kk = np.arange(P)[:, None]
    qq = np.arange(P)[None, :]
    for ktr in range(8):
        m[:, ktr * P:(ktr + 1) * P] = ((ktr * P + kk) <= (c * P + qq))
    return m.astype(ml_dtypes.bfloat16)


def _slot_rows(c):
    rows = []
    for b in range(B):
        for k in range(NCLS):
            j = 8 * k + c
            base = b * S + j * P
            rows.append((base, base + P))
    return rows


def kernel(L0_attn_x, L0_attn_qkv_weights, L0_ffn_w_fc, L0_ffn_b_fc,
           L0_ffn_w_proj, L0_ffn_b_proj):
    x = np.ascontiguousarray(np.asarray(L0_attn_x, dtype=np.float32).reshape(B * S, D))
    wqkv = np.ascontiguousarray(np.asarray(L0_attn_qkv_weights, dtype=np.float32))
    wfc = np.ascontiguousarray(np.asarray(L0_ffn_w_fc, dtype=np.float32))
    bfc = np.ascontiguousarray(np.asarray(L0_ffn_b_fc, dtype=np.float32).reshape(1, HID))
    wproj = np.ascontiguousarray(np.asarray(L0_ffn_w_proj, dtype=np.float32))
    bproj = np.ascontiguousarray(np.asarray(L0_ffn_b_proj, dtype=np.float32).reshape(1, D))

    if "nc" not in _CACHE:
        _CACHE["nc"] = build()
    nc = _CACHE["nc"]

    in_maps = []
    for c in range(NCORES):
        rows = _slot_rows(c)
        x_qc = np.concatenate([x[a:b_] for (a, b_) in rows], axis=0)
        in_maps.append({
            "x_kv": np.ascontiguousarray(x[(c % 2) * S:(c % 2 + 1) * S]),
            "x_q": np.ascontiguousarray(x_qc),
            "w_qkv": wqkv, "w_fc": wfc, "b_fc": bfc,
            "w_proj": wproj, "b_proj": bproj,
            "masks": _masks_for_core(c),
        })
    res = run_bass_kernel_spmd(nc, in_maps, core_ids=list(range(NCORES)))
    out = np.empty((B * S, D), dtype=np.float32)
    for c in range(NCORES):
        yc = res.results[c]["y"]
        for si, (a, b_) in enumerate(_slot_rows(c)):
            out[a:b_] = yc[si * P:(si + 1) * P]
    return out.reshape(B, S, D)


# version 24 (final)
# speedup vs baseline: 1.3122x; 1.0132x over previous
from contextlib import ExitStack

import numpy as np
import ml_dtypes
import concourse.bass as bass
import concourse.mybir as mybir
from concourse.bass_utils import run_bass_kernel_spmd

F32 = mybir.dt.float32
BF16 = mybir.dt.bfloat16
U32 = mybir.dt.uint32
AF = mybir.ActivationFunctionType
ALU = mybir.AluOpType
AX = mybir.AxisListType

NCORES = 8
P = 128
D = 1024
S = 8192
B = 2
H = 16
DH = 64
E3 = 3 * D
HID = 4096
NT = S // P
NCLS = 8
TPC = 2 * NCLS
TCORE = TPC * P
EXP_SHIFT = -25.0
PAIR_POLL = True
SHARED_KV = True
AMAX_CLAMP = 2.0 ** -96
MAGIC = float(1.5 * 2.0 ** 20)
EXPO_MASK = 0x7F800000


class Sem:
    def __init__(self, nc, name):
        self.h = nc.alloc_semaphore(name)
        self.n = 0

    def inc(self, ins, k=1):
        ins.then_inc(self.h, k)
        self.n += k
        return self.n


class AutoChain:

    _PASS = {"wait_ge", "engine", "bass"}

    def __init__(self, eng, sem):
        self._eng = eng
        self._sem = sem

    def __getattr__(self, name):
        f = getattr(self._eng, name)
        if name in AutoChain._PASS or not callable(f):
            return f
        sem = self._sem
        eng = self._eng

        def wrap(*a, **k):
            if sem.n:
                eng.wait_ge(sem.h, sem.n)
            ins = f(*a, **k)
            sem.inc(ins)
            return ins

        return wrap


def build():
    nc = bass.Bass("TRN2", target_bir_lowering=False)

    x_kv = nc.dram_tensor("x_kv", [S, D], F32, kind="ExternalInput").ap()
    x_q = nc.dram_tensor("x_q", [TCORE, D], F32, kind="ExternalInput").ap()
    w_qkv = nc.dram_tensor("w_qkv", [D, E3], F32, kind="ExternalInput").ap()
    w_fc = nc.dram_tensor("w_fc", [HID, D], F32, kind="ExternalInput").ap()
    b_fc = nc.dram_tensor("b_fc", [1, HID], F32, kind="ExternalInput").ap()
    w_proj = nc.dram_tensor("w_proj", [D, HID], F32, kind="ExternalInput").ap()
    b_proj = nc.dram_tensor("b_proj", [1, D], F32, kind="ExternalInput").ap()
    masks = nc.dram_tensor("masks", [P, 8 * P], BF16, kind="ExternalInput").ap()
    y = nc.dram_tensor("y", [TCORE, D], F32, kind="ExternalOutput").ap()

    _shr = {"addr_space": "Shared"} if SHARED_KV else {}
    kT_sh = nc.dram_tensor("kT_sh", [B * D, S], BF16, kind="Internal",
                           **_shr).ap()
    v_sh = nc.dram_tensor("v_sh", [B * S, H * 65], BF16, kind="Internal",
                          **_shr).ap()
    kT_loc = nc.dram_tensor("kT_loc", [D, S], BF16, kind="Internal").ap()
    v_loc = nc.dram_tensor("v_loc", [S, H * 65], BF16, kind="Internal").ap()
    kvflag_sh = nc.dram_tensor("kvflag_sh", [2, 8], F32, kind="Internal",
                               **_shr).ap()
    wfcqT_d = nc.dram_tensor("wfcqT_d", [D, HID], BF16, kind="Internal").ap()
    wprojqT_d = nc.dram_tensor("wprojqT_d", [HID, D], BF16, kind="Internal").ap()

    ident_d = nc.inline_tensor(np.eye(P, dtype=ml_dtypes.bfloat16), name="idbf").ap()
    ones_d = nc.inline_tensor(np.ones((1, P), np.float32), name="onef").ap()

    s_c = Sem(nc, "s_c")
    s_x = [Sem(nc, "s_x0"), Sem(nc, "s_x1")]
    s_wr = Sem(nc, "s_wr")
    s_br = Sem(nc, "s_br")
    s_wf = [Sem(nc, "s_wf0"), Sem(nc, "s_wf1"), Sem(nc, "s_wf2")]
    s_wp = [Sem(nc, "s_wp0"), Sem(nc, "s_wp1"), Sem(nc, "s_wp2"),
            Sem(nc, "s_wp3")]
    s_kv = [Sem(nc, "s_kv0"), Sem(nc, "s_kv1")]
    s_y = [Sem(nc, "s_y0"), Sem(nc, "s_y1")]
    s_wq = Sem(nc, "s_wq")
    s_sk = [Sem(nc, "s_sk0"), Sem(nc, "s_sk1")]
    s_sv = [Sem(nc, "s_sv0"), Sem(nc, "s_sv1")]
    s_wt = Sem(nc, "s_wt")
    s_wr2 = Sem(nc, "s_wr2")
    s_wt2 = Sem(nc, "s_wt2")
    s_pr = Sem(nc, "s_pr")
    s_bulk = Sem(nc, "s_bulk")
    s_flg = Sem(nc, "s_flg")

    sy, gp, te = nc.sync, nc.gpsimd, nc.tensor
    s_t = Sem(nc, "s_t")
    s_v = Sem(nc, "s_v")
    s_sc = Sem(nc, "s_sc")
    ve = AutoChain(nc.vector, s_v)
    sc = AutoChain(nc.scalar, s_sc)

    def W(eng, sem, val):
        if val > 0:
            eng.wait_ge(sem.h, val)

    ALL_SEMS = [s_c, s_x[0], s_x[1], s_wr, s_br, s_wf[0], s_wf[1], s_wf[2],
                s_wp[0], s_wp[1], s_wp[2], s_wp[3],
                s_kv[0], s_kv[1], s_y[0], s_y[1], s_wq, s_sk[0],
                s_sk[1], s_sv[0], s_sv[1], s_wt, s_wr2, s_wt2, s_pr, s_t,
                s_v, s_sc]

    def phase_join():
        for e in (sy, gp, te, nc.vector, nc.scalar):
            for sem in ALL_SEMS:
                W(e, sem, sem.n)

    ident_sb = nc.alloc_sbuf_tensor("ident_sb", [P, P], BF16)
    ones_sb = nc.alloc_sbuf_tensor("ones_sb", [1, P], F32)
    qT_sb = nc.alloc_sbuf_tensor("qT_sb", [P, 8 * TCORE], BF16)
    attn_sb = nc.alloc_sbuf_tensor("attn_sb", [P, TPC * D], BF16)
    ffn_sb = nc.alloc_sbuf_tensor("ffn_sb", [P, TPC * D], BF16)
    mr_sb = nc.alloc_sbuf_tensor("mr_sb", [P, 2 * TPC], F32)
    scr = {}
    flgsb = nc.alloc_sbuf_tensor("flgsb", [1, 8], F32)
    s_am = nc.alloc_sbuf_tensor("scr_am", [P, HID // 32], F32)
    s_mm = nc.alloc_sbuf_tensor("scr_m", [P, HID // 32], F32)
    s_s1 = nc.alloc_sbuf_tensor("scr_s1", [P, 32], F32)
    expb = nc.alloc_sbuf_tensor("expb", [P, 1], F32)

    ve.memset(expb[:], EXP_SHIFT)
    c_expb = s_v.n
    ld = sy.dma_start(ident_sb[:], ident_d)
    s_c.inc(ld, 16)
    ld = sy.dma_start(ones_sb[:], ones_d)
    s_c.inc(ld, 16)
    c_consts = s_c.n

    def quant_free(out_bf, in_f32, width):
        nb = width // 32
        x3 = in_f32.rearrange("p (b k) -> p b k", k=32)
        ve.tensor_reduce(s_am[:, 0:nb], x3, AX.X, ALU.max,
                         apply_absolute_value=True)
        ve.tensor_scalar(s_mm[:, 0:nb].bitcast(U32), s_am[:, 0:nb].bitcast(U32),
                         EXPO_MASK, None, ALU.bitwise_and)
        ve.tensor_scalar_mul(s_mm[:, 0:nb], s_mm[:, 0:nb], MAGIC)
        mb = s_mm[:, 0:nb].unsqueeze(2).broadcast_to((P, nb, 32))
        t3 = scr["a"][:, 0:width].rearrange("p (b k) -> p b k", k=32)
        o3 = out_bf.rearrange("p (b k) -> p b k", k=32)
        ve.tensor_tensor(t3, x3, mb, ALU.add)
        ve.tensor_tensor(o3, t3, mb, ALU.subtract)

    def stats_acc(tile_f32, sem_load, c_load, tt):
        W(sc, sem_load, c_load)
        sc.activation(scr["b"][:, 0:D], tile_f32, AF.Copy,
                      accum_out=s_s1[:, tt:tt + 1])
        sc.activation(scr["b"][:, 0:D], tile_f32, AF.Square,
                      accum_out=s_s1[:, 4 + tt:5 + tt])
        return s_sc.n

    def stats_finalize(nt, slot0_mr):
        ve.tensor_scalar_mul(s_s1[:, 8:8 + nt], s_s1[:, 0:nt], 1.0 / D)
        ve.tensor_scalar_mul(s_s1[:, 16:16 + nt], s_s1[:, 4:4 + nt], 1.0 / D)
        ve.tensor_tensor(s_s1[:, 20:20 + nt], s_s1[:, 8:8 + nt],
                         s_s1[:, 8:8 + nt], ALU.mult)
        ve.tensor_tensor(s_s1[:, 20:20 + nt], s_s1[:, 16:16 + nt],
                         s_s1[:, 20:20 + nt], ALU.subtract)
        ve.tensor_scalar_add(s_s1[:, 20:20 + nt], s_s1[:, 20:20 + nt], 1e-5)
        c_var = s_v.n
        W(sc, s_v, c_var)
        sc.sqrt(s_s1[:, 24:24 + nt], s_s1[:, 20:20 + nt])
        c_sd = s_sc.n
        W(ve, s_sc, c_sd)
        ve.reciprocal(s_s1[:, 12:12 + nt], s_s1[:, 24:24 + nt])
        if slot0_mr is not None:
            ve.tensor_copy(mr_sb[:].rearrange("p (s two) -> p s two", two=2)
                           [:, slot0_mr:slot0_mr + nt, 0:1],
                           s_s1[:, 8:8 + nt].unsqueeze(2))
            ve.tensor_copy(mr_sb[:].rearrange("p (s two) -> p s two", two=2)
                           [:, slot0_mr:slot0_mr + nt, 1:2],
                           s_s1[:, 12:12 + nt].unsqueeze(2))
        return s_v.n

    def stats_norm(tile_f32, sem_load, c_load, slot_mr, xn_out, prev_sb_reader_v):
        W(sc, sem_load, c_load)
        W(sc, s_v, prev_sb_reader_v)
        sc.square(scr["b"][:, 0:D], tile_f32)
        c_sq = s_sc.n
        W(ve, sem_load, c_load)
        ve.reduce_sum(s_s1[:, 0:1], tile_f32, AX.X)
        W(ve, s_sc, c_sq)
        ve.reduce_sum(s_s1[:, 1:2], scr["b"][:, 0:D], AX.X)
        ve.tensor_scalar_mul(s_s1[:, 2:3], s_s1[:, 0:1], 1.0 / D)
        ve.tensor_scalar_mul(s_s1[:, 3:4], s_s1[:, 1:2], 1.0 / D)
        ve.tensor_tensor(s_s1[:, 4:5], s_s1[:, 2:3], s_s1[:, 2:3], ALU.mult)
        ve.tensor_tensor(s_s1[:, 5:6], s_s1[:, 3:4], s_s1[:, 4:5], ALU.subtract)
        ve.tensor_scalar_add(s_s1[:, 5:6], s_s1[:, 5:6], 1e-5)
        c_var = s_v.n
        W(sc, s_v, c_var)
        sc.sqrt(s_s1[:, 6:7], s_s1[:, 5:6])
        c_sd = s_sc.n
        W(ve, s_sc, c_sd)
        ve.reciprocal(s_s1[:, 7:8], s_s1[:, 6:7])
        if slot_mr is not None:
            ve.tensor_copy(mr_sb[:, 2 * slot_mr:2 * slot_mr + 1], s_s1[:, 2:3])
            ve.tensor_copy(mr_sb[:, 2 * slot_mr + 1:2 * slot_mr + 2], s_s1[:, 7:8])
        ve.tensor_scalar(xn_out, tile_f32, s_s1[:, 2:3],
                         s_s1[:, 7:8], ALU.subtract, ALU.mult)
        return s_v.n, c_sq

    with nc.sbuf_tensor("wq_sb", [P, 8 * E3], BF16) as wq_sb:
        for dc in range(8):
            ins = gp.dma_start(wq_sb[:, dc * E3:(dc + 1) * E3],
                               w_qkv[dc * P:(dc + 1) * P, :])
            c_wq = s_wq.inc(ins, 16)

        with ExitStack() as _wst:
            _e = _wst.enter_context
            WMAX = 2048
            wraw = [_e(nc.sbuf_tensor(f"wraw{i}", [P, WMAX], F32))
                    for i in range(2)]
            scr["a"] = _e(nc.sbuf_tensor("scrWa", [P, D], F32))
            wqbf = [_e(nc.sbuf_tensor(f"wqbf{i}", [P, WMAX], BF16))
                    for i in range(2)]
            wtr = [_e(nc.sbuf_tensor(f"wtr{i}", [P, WMAX], BF16))
                   for i in range(2)]
            pw = [_e(nc.psum_tensor(f"pw{i}", [P, P], BF16)) for i in range(2)]
            tr_rel = [0, 0]
            s_wld = [s_wr, s_wr2]
            s_wst = [s_wt, s_wt2]
            c_store = [0, 0]
            c_vq = [0, 0]
            c_tr = [0, 0]
            wg = [0]

            def w_quant_transpose(src, rows, width, dst):
                dst3 = dst.rearrange("(c p) n -> p c n", p=P)
                for ch in range(rows // P):
                    for wo in range(0, width, WMAX):
                        pw_ = min(WMAX, width - wo)
                        p = wg[0] % 2
                        wg[0] += 1
                        W(sy, s_v, c_vq[p])
                        ins = sy.dma_start(
                            wraw[p][:, 0:pw_],
                            src[ch * P:(ch + 1) * P, wo:wo + pw_])
                        c_load = s_wld[p].inc(ins, 16)
                        W(ve, s_wld[p], c_load)
                        W(ve, s_t, c_tr[p])
                        for qo in range(0, pw_, D):
                            qw = min(D, pw_ - qo)
                            quant_free(wqbf[p][:, qo:qo + qw],
                                       wraw[p][:, qo:qo + qw], qw)
                        c_vq[p] = s_v.n
                        W(sc, s_wst[p], c_store[p])
                        for j in range(pw_ // P):
                            W(te, s_v, c_vq[p])
                            W(te, s_sc, tr_rel[j % 2])
                            c_tp = s_t.inc(te.transpose(
                                pw[j % 2][:], wqbf[p][:, j * P:(j + 1) * P],
                                ident_sb[:]))
                            W(sc, s_t, c_tp)
                            sc.activation(wtr[p][:, j * P:(j + 1) * P],
                                          pw[j % 2][:], AF.Copy)
                            tr_rel[j % 2] = s_sc.n
                        c_tr[p] = s_t.n
                        W(gp, s_sc, tr_rel[(pw_ // P - 1) % 2])
                        W(gp, s_sc, tr_rel[(pw_ // P) % 2])
                        ins = gp.dma_start(
                            dst3[:, wo // P:(wo + pw_) // P,
                                 ch * P:(ch + 1) * P],
                            wtr[p][:, 0:pw_].rearrange("p (c q) -> p c q", q=P))
                        c_store[p] = s_wst[p].inc(ins, 16)

            te.wait_ge(s_c.h, c_consts)
            w_quant_transpose(w_fc, HID, D, wfcqT_d)
            w_quant_transpose(w_proj, D, HID, wprojqT_d)
        c_wprep = (s_wt.n, s_wt2.n)

        phase_join()
        with ExitStack() as _kst:
            _e = _kst.enter_context
            xf = [_e(nc.sbuf_tensor(f"xf{i}", [P, D], F32)) for i in range(4)]
            scr["b"] = _e(nc.sbuf_tensor("scrKb", [P, D], F32))
            xn = [_e(nc.sbuf_tensor(f"xn{i}", [P, D], BF16)) for i in range(4)]
            xT = [_e(nc.sbuf_tensor(f"xT{i}", [P, 8 * 512], BF16)) for i in range(2)]
            kst = [_e(nc.sbuf_tensor(f"kst{i}", [P, 512], BF16)) for i in range(2)]
            vst = [_e(nc.sbuf_tensor(f"vst{i}", [P, H * 65], BF16)) for i in range(2)]
            pt = [_e(nc.psum_tensor(f"pt{i}", [P, 8 * P], BF16)) for i in range(2)]
            pk = [_e(nc.psum_tensor(f"pk{i}", [P, 512], F32)) for i in range(2)]
            pv = [_e(nc.psum_tensor(f"pv{i}", [P, 512], F32)) for i in range(2)]

            ve.memset(
                vst[0][:].rearrange("p (h c) -> p h c", c=65)[:, :, 64:65], 1.0)
            ve.memset(
                vst[1][:].rearrange("p (h c) -> p h c", c=65)[:, :, 64:65], 1.0)
            c_vones = s_v.n

            rel_xf_sc = [0, 0]
            rel_xf_v = [0, 0]
            rel_xn = [0, 0, 0, 0]
            rel_xT = [0, 0]
            rel_kst = [0, 0]
            rel_vst = [c_vones, c_vones]
            rel_vst_gd = [0, 0]
            tr_rel = [0, 0]
            kev = [0, 0]
            vev = [0, 0]
            prev_sb_reader = 0

            pid_rv = nc.partition_id([mybir.EngineType.Pool])
            rb = gp.scalar_reg_alu(ALU.mod, pid_rv, 2)
            rkbase = gp.scalar_reg_alu(ALU.mult, rb, D)
            rvbase = gp.scalar_reg_alu(ALU.mult, rb, S)
            ve.memset(flgsb[:], 0.0)

            for ch in range(NT * P // 512):
                tloc = ch
                cpar = ch % 2
                for tt in range(4):
                    i = ch * 4 + tt
                    if tt == 0:
                        W(sy, s_sc, rel_xf_sc[0])
                        W(sy, s_v, rel_xf_v[0])
                    ins = sy.dma_start(xf[tt][:], x_kv[i * P:(i + 1) * P, :])
                    c_load4 = s_x[cpar].inc(ins, 16)
                for tt in range(4):
                    stats_acc(xf[tt][:], s_x[cpar], c_load4, tt)
                c_acc = s_sc.n
                W(ve, s_sc, c_acc)
                stats_finalize(4, None)
                rel_xf_sc[0] = s_sc.n
                for tt in range(4):
                    W(ve, s_t, rel_xn[tt])
                    ve.tensor_scalar(xn[tt][:], xf[tt][:],
                                     s_s1[:, 8 + tt:9 + tt],
                                     s_s1[:, 12 + tt:13 + tt],
                                     ALU.subtract, ALU.mult)
                c_norm = s_v.n
                rel_xf_v[0] = s_v.n
                for tt in range(4):
                    bk = tt % 2
                    W(te, s_v, c_norm)
                    W(te, s_v, tr_rel[bk])
                    W(te, s_t, rel_xT[cpar])
                    for dc in range(8):
                        c_tp = s_t.inc(te.transpose(pt[bk][:, dc * P:(dc + 1) * P],
                                                    xn[tt][:, dc * P:(dc + 1) * P],
                                                    ident_sb[:]))
                    rel_xn[tt] = s_t.n
                    W(ve, s_t, c_tp)
                    ve.tensor_copy(
                        xT[cpar][:].rearrange("p (c f) -> p c f", f=512)
                        [:, :, tt * P:(tt + 1) * P],
                        pt[bk][:].rearrange("p (c q) -> p c q", q=P))
                    tr_rel[bk] = s_v.n
                c_xT = s_v.n

                W(te, s_wq, c_wq)
                for ec in range(8):
                    kp = ec % 2
                    W(te, s_v, c_xT)
                    W(te, s_v, kev[kp])
                    for dc in range(8):
                        mm = te.matmul(
                            pk[kp][:],
                            wq_sb[:, dc * E3 + D + ec * P: dc * E3 + D + (ec + 1) * P],
                            xT[cpar][:, dc * 512:(dc + 1) * 512],
                            start=(dc == 0), stop=(dc == 7))
                    c_kmm = s_t.inc(mm)
                    W(ve, s_t, c_kmm)
                    W(ve, s_sk[kp], rel_kst[kp])
                    ve.tensor_copy(kst[kp][:], pk[kp][:])
                    kev[kp] = s_v.n
                    W(gp, s_v, kev[kp])
                    ins = gp.dma_start(
                        kT_loc[ec * P:(ec + 1) * P, tloc * 512:(tloc + 1) * 512],
                        kst[kp][:])
                    rel_kst[kp] = s_sk[kp].inc(ins, 16)

                for tt in range(4):
                    vp = tt % 2
                    for nh in range(2):
                        W(te, s_v, c_xT)
                        W(te, s_v, vev[vp])
                        for dc in range(8):
                            mm = te.matmul(
                                pv[vp][:],
                                xT[cpar][:, dc * 512 + tt * P: dc * 512 + (tt + 1) * P],
                                wq_sb[:, dc * E3 + 2 * D + nh * 512:
                                      dc * E3 + 2 * D + (nh + 1) * 512],
                                start=(dc == 0), stop=(dc == 7))
                        c_vmm = s_t.inc(mm)
                        W(ve, s_t, c_vmm)
                        W(ve, s_sv[vp], rel_vst_gd[vp])
                        ve.tensor_copy(
                            vst[vp][:].rearrange("p (h c) -> p h c", c=65)
                            [:, nh * 8:(nh + 1) * 8, 0:64],
                            pv[vp][:].rearrange("p (h e) -> p h e", e=64))
                        vev[vp] = s_v.n
                    W(gp, s_v, vev[vp])
                    ins = gp.dma_start(
                        v_loc[(tloc * 4 + tt) * P:(tloc * 4 + tt + 1) * P, :],
                        vst[vp][:])
                    rel_vst_gd[vp] = s_sv[vp].inc(ins, 16)
                rel_xT[cpar] = s_t.n

        c_kvdone = (s_sk[0].n, s_sk[1].n, s_sv[0].n, s_sv[1].n)
        W(gp, s_sk[0], c_kvdone[0])
        W(gp, s_sk[1], c_kvdone[1])
        W(gp, s_sv[0], c_kvdone[2])
        W(gp, s_sv[1], c_kvdone[3])
        ins = gp.dma_start(kT_sh[bass.ds(rkbase, D)], kT_loc)
        s_bulk.inc(ins, 16)
        ins = gp.dma_start(v_sh[bass.ds(rvbase, S)], v_loc)
        c_bulk = s_bulk.inc(ins, 16)
        gp.wait_ge(s_bulk.h, c_bulk)
        gp.wait_ge(s_c.h, c_consts)
        s_flg.inc(gp.dma_start(kvflag_sh[bass.ds(rb, 1), :],
                               ones_sb[:, 0:8]), 16)
        phase_join()
        with ExitStack() as _qst:
            _e = _qst.enter_context
            qf = [_e(nc.sbuf_tensor(f"qf{i}", [P, D], F32)) for i in range(4)]
            scr["b"] = _e(nc.sbuf_tensor("scrQb", [P, D], F32))
            qn = [_e(nc.sbuf_tensor(f"qn{i}", [P, D], BF16)) for i in range(4)]
            qTb = [_e(nc.sbuf_tensor(f"qTb{i}", [P, 8 * 512], BF16)) for i in range(2)]
            qt = [_e(nc.psum_tensor(f"qt{i}", [P, 8 * P], BF16)) for i in range(2)]
            pq = [_e(nc.psum_tensor(f"pq{i}", [P, 512], F32)) for i in range(2)]
            rel_qf_sc = [s_sc.n, s_sc.n]
            rel_qf_v = [s_v.n, s_v.n]
            rel_qn = [s_t.n] * 4
            rel_qT = [s_t.n, s_t.n]
            tr_rel = [s_v.n, s_v.n]
            qev = [0, 0]
            prev_sb_reader = s_v.n
            for chq in range(4):
                cpar = chq % 2
                for tt in range(4):
                    slot = chq * 4 + tt
                    if tt == 0:
                        W(sy, s_sc, rel_qf_sc[0])
                        W(sy, s_v, rel_qf_v[0])
                    ins = sy.dma_start(qf[tt][:], x_q[slot * P:(slot + 1) * P, :])
                    c_load4 = s_x[cpar].inc(ins, 16)
                for tt in range(4):
                    stats_acc(qf[tt][:], s_x[cpar], c_load4, tt)
                c_acc = s_sc.n
                W(ve, s_sc, c_acc)
                stats_finalize(4, chq * 4)
                rel_qf_sc[0] = s_sc.n
                for tt in range(4):
                    W(ve, s_t, rel_qn[tt])
                    ve.tensor_scalar(qn[tt][:], qf[tt][:],
                                     s_s1[:, 8 + tt:9 + tt],
                                     s_s1[:, 12 + tt:13 + tt],
                                     ALU.subtract, ALU.mult)
                c_norm = s_v.n
                rel_qf_v[0] = s_v.n
                for tt in range(4):
                    bk = tt % 2
                    W(te, s_v, c_norm)
                    W(te, s_v, tr_rel[bk])
                    W(te, s_t, rel_qT[cpar])
                    for dc in range(8):
                        c_tp = s_t.inc(te.transpose(qt[bk][:, dc * P:(dc + 1) * P],
                                                    qn[tt][:, dc * P:(dc + 1) * P],
                                                    ident_sb[:]))
                    rel_qn[tt] = s_t.n
                    W(ve, s_t, c_tp)
                    ve.tensor_copy(
                        qTb[cpar][:].rearrange("p (c f) -> p c f", f=512)
                        [:, :, tt * P:(tt + 1) * P],
                        qt[bk][:].rearrange("p (c q) -> p c q", q=P))
                    tr_rel[bk] = s_v.n
                c_qT = s_v.n
                for ec in range(8):
                    qp = ec % 2
                    W(te, s_v, c_qT)
                    W(te, s_v, qev[qp])
                    for dc in range(8):
                        mm = te.matmul(
                            pq[qp][:],
                            wq_sb[:, dc * E3 + ec * P: dc * E3 + (ec + 1) * P],
                            qTb[cpar][:, dc * 512:(dc + 1) * 512],
                            start=(dc == 0), stop=(dc == 7))
                    c_qmm = s_t.inc(mm)
                    W(ve, s_t, c_qmm)
                    ve.tensor_scalar(
                        qT_sb[:, ec * TCORE + chq * 512: ec * TCORE + (chq + 1) * 512],
                        pq[qp][:], 0.125, None, ALU.mult)
                    qev[qp] = s_v.n
                rel_qT[cpar] = s_t.n

    phase_join()
    with ExitStack() as _fst:
        _e = _fst.enter_context
        bfc_hl = _e(nc.sbuf_tensor("bfc_hl", [2, HID], BF16))
        bpr_hl = _e(nc.sbuf_tensor("bpr_hl", [2, D], BF16))
        ones2 = _e(nc.sbuf_tensor("ones2", [2, P], BF16))
        bfrow = _e(nc.sbuf_tensor("bfrow", [1, D], F32))
        blo = _e(nc.sbuf_tensor("blo", [1, D], BF16))
        fx = [_e(nc.sbuf_tensor(f"fx{i}", [P, D], F32)) for i in range(2)]
        fxn = _e(nc.sbuf_tensor("fxn", [P, D], F32))
        scr["a"] = _e(nc.sbuf_tensor("scrFa", [P, D], F32))
        fxq = [_e(nc.sbuf_tensor(f"fxq{i}", [P, D], BF16)) for i in range(2)]
        xqT = _e(nc.sbuf_tensor("xqT", [P, 2 * D], BF16))
        wfcb = [_e(nc.sbuf_tensor(f"wfcb{i}", [P, 8 * 512], BF16)) for i in range(3)]
        hgl = [_e(nc.sbuf_tensor(f"hgl{i}", [P, 1024], F32)) for i in range(2)]
        hqb = [_e(nc.sbuf_tensor(f"hqb{i}", [P, 1024], BF16)) for i in range(2)]
        hqT = _e(nc.sbuf_tensor("hqT", [P, 32 * 256], BF16))
        wprb = [_e(nc.sbuf_tensor(f"wprb{i}", [P, D], BF16)) for i in range(4)]

        ve.memset(ones2[:], 1.0)
        c_b_gate = 0
        c_lo = 0
        for bsrc, hl, width in ((b_fc, bfc_hl, HID), (b_proj, bpr_hl, D)):
            for qo in range(0, width, D):
                W(sy, s_v, c_b_gate)
                ins = sy.dma_start(bfrow[:], bsrc[:, qo:qo + D])
                c_br = s_br.inc(ins, 16)
                W(ve, s_br, c_br)
                W(ve, s_wq, c_lo)
                ve.tensor_copy(hl[0:1, qo:qo + D], bfrow[:])
                ve.tensor_copy(scr["a"][0:1, :], hl[0:1, qo:qo + D])
                ve.tensor_tensor(blo[:], bfrow[:], scr["a"][0:1, :],
                                 ALU.subtract)
                c_b_gate = s_v.n
                W(gp, s_v, c_b_gate)
                ins = gp.dma_start(hl[1:2, qo:qo + D], blo[:])
                c_lo = s_wq.inc(ins, 16)
        c_bias_ready = s_v.n

        with ExitStack() as _fps:
            _e2 = _fps.enter_context
            pf = [_e2(nc.psum_tensor(f"pf{i}", [P, 1024], F32)) for i in range(2)]
            pp2 = _e2(nc.psum_tensor("pp2", [P, 2048], F32))
            trb = [pp2[:, 0:512].bitcast(BF16), pp2[:, 512:1024].bitcast(BF16)]
            tr_rel = [s_v.n, s_v.n]
            rel_fx_v = [s_v.n, s_v.n]
            rel_fxq = [s_t.n, s_t.n]
            fev = [0, 0]
            rel_wfc = [s_t.n] * 3
            rel_wpr = [s_t.n] * 4
            c_wfc_ld = [0, 0, 0]
            c_wpr_ld = [0, 0, 0, 0]
            rel_hgl_v = [0, 0]
            c_gelu = [0, 0]
            rel_hqb_t = [s_t.n, s_t.n]
            rel_xqT = s_t.n
            rel_hqT = s_t.n
            c_pp_ev = 0

            def xq_prep(sc0):
                cqs = []
                for tt in range(2):
                    slot = sc0 * 2 + tt
                    par = tt % 2
                    W(sy, s_v, rel_fx_v[par])
                    ins = sy.dma_start(fx[par][:],
                                       x_q[slot * P:(slot + 1) * P, :])
                    c_load = s_x[par].inc(ins, 16)
                    W(ve, s_x[par], c_load)
                    ve.tensor_scalar(fxn[:], fx[par][:],
                                     mr_sb[:, 2 * slot:2 * slot + 1],
                                     mr_sb[:, 2 * slot + 1:2 * slot + 2],
                                     ALU.subtract, ALU.mult)
                    rel_fx_v[par] = s_v.n
                    W(ve, s_t, rel_fxq[par])
                    quant_free(fxq[par][:], fxn[:], D)
                    cqs.append(s_v.n)
                return cqs

            cqs = xq_prep(0)
            for sc0 in range(8):
                for tt in range(2):
                    par = tt % 2
                    bk = tt % 2
                    W(te, s_v, cqs[tt])
                    W(te, s_v, tr_rel[bk])
                    W(te, s_v, c_pp_ev)
                    W(te, s_t, rel_xqT)
                    for dc in range(8):
                        c_tp = s_t.inc(te.transpose(trb[bk][:, dc * P:(dc + 1) * P],
                                                    fxq[par][:, dc * P:(dc + 1) * P],
                                                    ident_sb[:]))
                    W(ve, s_t, c_tp)
                    ve.tensor_copy(xqT[:, tt * D:(tt + 1) * D], trb[bk][:])
                    tr_rel[bk] = s_v.n
                    rel_fxq[par] = s_t.n
                c_xqT = s_v.n

                def emit_hq_transposes(pn, pfp, c_hq_p):
                    bk3 = pn % 2
                    W(te, s_v, c_hq_p)
                    W(te, s_v, tr_rel[bk3])
                    c_tp = None
                    for j in range(8):
                        c_tp = s_t.inc(te.transpose(
                            trb[bk3][:, j * P:(j + 1) * P],
                            hqb[pfp][:, j * P:(j + 1) * P],
                            ident_sb[:]))
                    rel_hqb_t[pfp] = s_t.n
                    W(ve, s_t, c_tp)
                    W(ve, s_t, rel_hqT)
                    for tt in range(2):
                        ve.tensor_copy(
                            hqT[:].rearrange("p (c f) -> p c f", f=256)
                            [:, pn * 4:(pn + 1) * 4, tt * P:(tt + 1) * P],
                            trb[bk3][:, tt * 512:(tt + 1) * 512]
                            .rearrange("p (c q) -> p c q", q=P))
                    tr_rel[bk3] = s_v.n

                pending = None
                for nch in range(8):
                    wp = nch % 3
                    fp_ = nch % 2
                    W(gp, s_t, rel_wfc[wp])
                    ins = gp.dma_start(
                        wfcb[wp][:].rearrange("p (c n) -> p c n", n=512),
                        wfcqT_d.rearrange("(c p) n -> p c n", p=P)
                        [:, :, nch * 512:(nch + 1) * 512])
                    c_wfc_ld[wp] = s_wf[wp].inc(ins, 16)
                    if nch == 0:
                        W(te, s_v, c_xqT)
                    W(te, s_wf[wp], c_wfc_ld[wp])
                    W(te, s_sc, fev[fp_])
                    if sc0 == 0 and nch == 0:
                        W(te, s_v, c_bias_ready)
                        W(te, s_wq, c_lo)
                    for tt in range(2):
                        for dc in range(8):
                            te.matmul(
                                pf[fp_][:, tt * 512:(tt + 1) * 512],
                                xqT[:, tt * D + dc * P: tt * D + (dc + 1) * P],
                                wfcb[wp][:, dc * 512:(dc + 1) * 512],
                                start=(dc == 0), stop=False)
                        mm = te.matmul(
                            pf[fp_][:, tt * 512:(tt + 1) * 512],
                            ones2[:], bfc_hl[:, nch * 512:(nch + 1) * 512],
                            start=False, stop=True)
                    c_fmm = s_t.inc(mm)
                    rel_wfc[wp] = s_t.n
                    if pending is not None:
                        emit_hq_transposes(*pending)
                    W(sc, s_t, c_fmm)
                    W(sc, s_v, rel_hgl_v[fp_])
                    sc.activation(hgl[fp_][:], pf[fp_][:], AF.Gelu)
                    c_gelu[fp_] = s_sc.n
                    fev[fp_] = c_gelu[fp_]
                    W(ve, s_sc, c_gelu[fp_])
                    W(ve, s_t, rel_hqb_t[fp_])
                    quant_free(hqb[fp_][:], hgl[fp_][:], 1024)
                    c_hq = s_v.n
                    rel_hgl_v[fp_] = c_hq
                    pending = (nch, fp_, c_hq)
                emit_hq_transposes(*pending)
                c_hqT = s_v.n
                if sc0 + 1 < 8:
                    cqs = xq_prep(sc0 + 1)

                for ncr in range(32):
                    wp = ncr % 4
                    W(gp, s_t, rel_wpr[wp])
                    ins = gp.dma_start(wprb[wp][:],
                                       wprojqT_d[ncr * P:(ncr + 1) * P, :])
                    c_wpr_ld[wp] = s_wp[wp].inc(ins, 16)
                    W(te, s_wp[wp], c_wpr_ld[wp])
                    if ncr == 0:
                        W(te, s_v, c_hqT)
                        W(te, s_v, tr_rel[0])
                        W(te, s_v, tr_rel[1])
                        for tt in range(2):
                            for mh in range(2):
                                te.matmul(
                                    pp2[:, tt * 1024 + mh * 512:
                                        tt * 1024 + (mh + 1) * 512],
                                    ones2[:], bpr_hl[:, mh * 512:(mh + 1) * 512],
                                    start=True, stop=False,
                                    skip_group_check=True)
                    for tt in range(2):
                        for mh in range(2):
                            mm = te.matmul(
                                pp2[:, tt * 1024 + mh * 512:
                                    tt * 1024 + (mh + 1) * 512],
                                hqT[:, ncr * 256 + tt * P:
                                    ncr * 256 + (tt + 1) * P],
                                wprb[wp][:, mh * 512:(mh + 1) * 512],
                                start=False, stop=(ncr == 31),
                                skip_group_check=True)
                    rel_wpr[wp] = s_t.inc(mm)
                c_proj = s_t.n
                W(ve, s_t, c_proj)
                for rr in range(4):
                    ve.tensor_copy(
                        ffn_sb[:, sc0 * 2 * D + rr * 512:
                               sc0 * 2 * D + (rr + 1) * 512],
                        pp2[:, rr * 512:(rr + 1) * 512])
                c_pp_ev = s_v.n
                rel_xqT = s_t.n
                rel_hqT = s_t.n

    rpn = gp.scalar_reg_alu(ALU.add, pid_rv, 1)
    rr_ = gp.scalar_reg_alu(ALU.mod, rpn, 2)
    if PAIR_POLL:
        for _ in range(32):
            s_flg.inc(gp.dma_start(flgsb[:], kvflag_sh[bass.ds(rr_, 1), :]),
                      16)
        rf = gp.alloc_register()
        gp.reg_load(rf, flgsb[0:1, 0:1].bitcast(U32))
        with gp.While(rf == 0):
            gp.reg_load(rf, flgsb[0:1, 0:1].bitcast(U32))
        gp.sem_inc(s_pr.h, 1)
    else:
        ins = gp.dma_start(flgsb[:], kvflag_sh[bass.ds(rr_, 1), :])
        c_fl = s_flg.inc(ins, 16)
        gp.wait_ge(s_flg.h, c_fl)
        gp.sem_inc(s_pr.h, 1)

    phase_join()
    with ExitStack() as _ast:
        _e = _ast.enter_context
        kbuf = [_e(nc.sbuf_tensor(f"kbuf{i}", [P, S], BF16)) for i in range(2)]
        kraw = _e(nc.sbuf_tensor("kraw", [P, S], BF16))
        masks_sb = _e(nc.sbuf_tensor("masks_sb", [P, 8 * P], BF16))
        ld = sy.dma_start(masks_sb[:], masks)
        c_masks = s_c.inc(ld, 16)
        vbuf = [_e(nc.sbuf_tensor(f"vbuf{i}", [P, NT * 130], BF16)) for i in range(2)]
        pT = [_e(nc.sbuf_tensor(f"pTb{i}", [P, 2048], BF16)) for i in range(5)]
        pTd = [_e(nc.sbuf_tensor(f"pTd{i}", [P, 256], BF16)) for i in range(6)]
        sps = _e(nc.psum_tensor("sps", [P, 2048], F32))
        avp = _e(nc.psum_tensor("avp", [P, 2048], F32))

        def av_off(sh):
            return 128 * sh

        c_kvload = [0, 0]
        rel_kv = [0, 0]
        c_avev = 0
        c_exp = {}
        c_mask = {}
        c_avt = {}
        dg_rel0 = 0

        def issue_kv_load(bhp, rel_kraw_t):
            b, hp = bhp // 8, bhp % 8
            par = bhp % 2
            W(sy, s_t, rel_kv[par])
            W(sy, s_t, rel_kraw_t)
            W(sy, s_sk[0], c_kvdone[0])
            W(sy, s_sk[1], c_kvdone[1])
            W(sy, s_sv[0], c_kvdone[2])
            W(sy, s_sv[1], c_kvdone[3])
            kview = kT_sh[b * D:(b + 1) * D].rearrange("c (j dd) -> j c dd",
                                                       dd=64)
            for cg in range(8):
                ins = sy.dma_start(
                    kraw[:].rearrange("j (c dd) -> j c dd", dd=64)
                    [:, cg * 16:(cg + 1) * 16, :],
                    kview[:, hp * P + cg * 16: hp * P + (cg + 1) * 16, :])
                s_kv[par].inc(ins, 16)
            c = 0
            for g in range(8):
                ins = sy.dma_start(
                    vbuf[par][:, g * 1040:(g + 1) * 1040]
                    .rearrange("p (k c) -> p k c", c=130),
                    v_sh[b * S:(b + 1) * S]
                    .rearrange("(g k p) c -> g p k c", p=P, k=8)
                    [g, :, :, 2 * hp * 65: 2 * hp * 65 + 130])
                c = s_kv[par].inc(ins, 16)
            c_kvload[par] = c

        ve.wait_ge(s_c.h, c_masks)
        sy.wait_ge(s_pr.h, 1)
        sy.wait_ge(s_bulk.h, c_bulk)
        rel_kraw = 0
        c_exp_last = 0
        issue_kv_load(0, 0)
        for bhp in range(16):
            b, hp = bhp // 8, bhp % 8
            par = bhp % 2

            W(te, s_kv[par], c_kvload[par])
            W(te, s_sc, c_exp_last)
            trb = [sps[:, 0:512].bitcast(BF16), sps[:, 1024:1536].bitcast(BF16)]
            c_trev = {-2: 0, -1: 0}
            for g in range(16):
                bk = g % 2
                W(te, s_v, c_trev[g - 2])
                for i in range(4):
                    kt = g * 4 + i
                    for hl in range(2):
                        mm = te.transpose(
                            trb[bk][hl * 64:(hl + 1) * 64, i * P:(i + 1) * P],
                            kraw[:, (hl * 64 + kt) * 64:(hl * 64 + kt + 1) * 64],
                            ident_sb[:])
                c_tr = s_t.inc(mm)
                W(ve, s_t, c_tr)
                ve.tensor_copy(kbuf[par][:, g * 512:(g + 1) * 512],
                               trb[bk][:, 0:512])
                c_trev[g] = s_v.n
            rel_kraw = s_t.n
            c_kbuf = s_v.n

            if bhp + 1 < 16:
                issue_kv_load(bhp + 1, rel_kraw)
            c_exp.clear()
            c_mask.clear()
            c_avt.clear()
            for dd in range(1, 7):
                c_exp[-dd] = c_exp_last
                c_avt[-dd] = s_t.n
            md = [0]
            pTd_idx = {}
            dg_rel = [dg_rel0] * 6
            avp_started = set()
            qbase = hp * TCORE + b * 1024

            stages = []
            for kt in range(NT):
                c0 = (kt // 8) * P
                if c0 < 512:
                    stages.append((kt, c0, 512))
                    stages.append((kt, 512, 1024))
                else:
                    stages.append((kt, c0, 1024))
            nstg = len(stages)

            def _av_mm(si, sp, kt, hh, cl):
                sh = hh * 8 + cl
                off = av_off(sh)
                bk = sh // 4
                st = bk not in avp_started
                avp_started.add(bk)
                return te.matmul(
                    avp[:, off:off + 65],
                    pT[sp][:, hh * 1024 + cl * P: hh * 1024 + (cl + 1) * P],
                    vbuf[par][:, kt * 130 + hh * 65: kt * 130 + (hh + 1) * 65],
                    start=st, stop=(kt == 8 * (cl + 1) - 1),
                    skip_group_check=True)

            def emit_av_nd(si, skip_exp_wait=False):
                kt, a0, a1 = stages[si]
                sp = si % 5
                start = kt // 8
                diag = a0 <= start * P < a1
                if not skip_exp_wait:
                    W(te, s_sc, c_exp[si])
                mm = None
                for hh in range(2):
                    for cl in range(max(start, a0 // P), a1 // P):
                        if diag and cl == start:
                            continue
                        mm = _av_mm(si, sp, kt, hh, cl)
                c_avt[si] = s_t.inc(mm) if mm is not None else s_t.n

            def emit_av_dg(si):
                kt, a0, a1 = stages[si]
                start = kt // 8
                if not (a0 <= start * P < a1):
                    return
                W(te, s_v, c_mask[si])
                pdx = pTd_idx[si]
                for hh in range(2):
                    sh = hh * 8 + start
                    off = av_off(sh)
                    bk = sh // 4
                    st = bk not in avp_started
                    avp_started.add(bk)
                    mm = te.matmul(
                        avp[:, off:off + 65],
                        pTd[pdx][:, hh * P:(hh + 1) * P],
                        vbuf[par][:, kt * 130 + hh * 65: kt * 130 + (hh + 1) * 65],
                        start=st, stop=(kt == 8 * (start + 1) - 1),
                        skip_group_check=True)
                dg_rel[pdx] = s_t.inc(mm)

            for si in range(nstg):
                kt, a0, a1 = stages[si]
                sp2 = si % 2
                sp = si % 5
                start = kt // 8
                lo = a0 % 512
                width = a1 - a0
                W(te, s_sc, c_exp[si - 2])
                if si == 0:
                    W(te, s_kv[par], c_kvload[par])
                    W(te, s_v, c_avev)
                    W(te, s_v, c_kbuf)
                for hh in range(2):
                    mm = te.matmul(
                        sps[:, hh * 1024 + sp2 * 512 + lo:
                            hh * 1024 + sp2 * 512 + lo + width],
                        kbuf[par][hh * 64:(hh + 1) * 64, kt * P:(kt + 1) * P],
                        qT_sb[hh * 64:(hh + 1) * 64, qbase + a0: qbase + a1],
                        start=True, stop=True)
                c_qk = s_t.inc(mm)
                if si >= 4:
                    emit_av_nd(si - 4, skip_exp_wait=True)
                    emit_av_dg(si - 4)
                W(sc, s_t, c_qk)
                W(sc, s_t, c_avt[si - 5])
                src = sps[:].rearrange("p (h q) -> p h q", h=2)\
                    [:, :, sp2 * 512 + lo: sp2 * 512 + lo + width]
                dst = pT[sp][:].rearrange("p (h q) -> p h q", h=2)[:, :, a0:a1]
                sc.activation(dst, src, AF.Exp, bias=expb[:], scale=1.0)
                c_exp[si] = s_sc.n
                c_exp_last = c_exp[si]
                if a0 <= start * P < a1:
                    pdx = md[0] % 6
                    md[0] += 1
                    pTd_idx[si] = pdx
                    W(ve, s_sc, c_exp[si])
                    W(ve, s_t, dg_rel[pdx])
                    ve.tensor_tensor(
                        pTd[pdx][:].rearrange("p (h q) -> p h q", h=2),
                        pT[sp][:].rearrange("p (h q) -> p h q", h=2)
                        [:, :, start * P:(start + 1) * P],
                        masks_sb[:, (kt % 8) * P:(kt % 8 + 1) * P]
                        .unsqueeze(1).broadcast_to((P, 2, P)),
                        ALU.mult)
                    c_mask[si] = s_v.n

            for fsi in (nstg - 4, nstg - 3, nstg - 2, nstg - 1):
                emit_av_nd(fsi)
                emit_av_dg(fsi)
            c_av_last = s_t.n
            dg_rel0 = s_t.n
            rel_kv[par] = c_av_last

            W(ve, s_t, c_av_last)
            ve.reciprocal(
                s_s1[:, 0:16].unsqueeze(2),
                avp[:].rearrange("p (s c) -> p s c", c=128)[:, :, 64:65])
            dst = attn_sb[:].rearrange("p (s hh e) -> p s hh e", hh=16, e=64)\
                [:, b * 8:(b + 1) * 8, 2 * hp:2 * hp + 2, :]
            src = avp[:].rearrange("p (hh cl c) -> p cl hh c", hh=2, c=128)\
                [:, :, :, 0:64]
            rcp = s_s1[:, 0:16].rearrange("p (hh cl) -> p cl hh", hh=2)\
                .unsqueeze(3).broadcast_to((P, 8, 2, 64))
            ve.tensor_tensor(dst, src, rcp, ALU.mult)
            c_avev = s_v.n

    phase_join()
    with ExitStack() as _rst:
        _e = _rst.enter_context
        rx = [_e(nc.sbuf_tensor(f"rx{i}", [P, D], F32)) for i in range(2)]
        ro = [_e(nc.sbuf_tensor(f"ro{i}", [P, D], F32)) for i in range(2)]
        scr["a"] = _e(nc.sbuf_tensor("scrRa", [P, D], F32))
        scr["b"] = _e(nc.sbuf_tensor("scrRb", [P, D], F32))
        rel_rx = [s_v.n, s_v.n]
        rel_ro = [0, 0]
        for slot in range(TPC):
            par = slot % 2
            W(sy, s_v, rel_rx[par])
            ins = sy.dma_start(rx[par][:], x_q[slot * P:(slot + 1) * P, :])
            c_load = s_x[par].inc(ins, 16)
            W(ve, s_x[par], c_load)
            W(ve, s_y[par], rel_ro[par])
            ve.tensor_copy(scr["a"][:, 0:D], attn_sb[:, slot * D:(slot + 1) * D])
            ve.tensor_copy(scr["b"][:, 0:D], ffn_sb[:, slot * D:(slot + 1) * D])
            ve.tensor_tensor(ro[par][:], rx[par][:], scr["a"][:, 0:D], ALU.add)
            ve.tensor_tensor(ro[par][:], ro[par][:], scr["b"][:, 0:D], ALU.add)
            c_ro = s_v.n
            rel_rx[par] = c_ro
            W(sy, s_v, c_ro)
            ins = sy.dma_start(y[slot * P:(slot + 1) * P, :], ro[par][:])
            rel_ro[par] = s_y[par].inc(ins, 16)
        sy.wait_ge(s_y[0].h, s_y[0].n)
        sy.wait_ge(s_y[1].h, s_y[1].n)

    return nc


_CACHE = {}


def _masks_for_core(c):
    m = np.zeros((P, 8 * P), dtype=np.float32)
    kk = np.arange(P)[:, None]
    qq = np.arange(P)[None, :]
    for ktr in range(8):
        m[:, ktr * P:(ktr + 1) * P] = ((ktr * P + kk) <= (c * P + qq))
    return m.astype(ml_dtypes.bfloat16)


def _slot_rows(c):
    rows = []
    for b in range(B):
        for k in range(NCLS):
            j = 8 * k + c
            base = b * S + j * P
            rows.append((base, base + P))
    return rows


def kernel(L0_attn_x, L0_attn_qkv_weights, L0_ffn_w_fc, L0_ffn_b_fc,
           L0_ffn_w_proj, L0_ffn_b_proj):
    x = np.ascontiguousarray(np.asarray(L0_attn_x, dtype=np.float32).reshape(B * S, D))
    wqkv = np.ascontiguousarray(np.asarray(L0_attn_qkv_weights, dtype=np.float32))
    wfc = np.ascontiguousarray(np.asarray(L0_ffn_w_fc, dtype=np.float32))
    bfc = np.ascontiguousarray(np.asarray(L0_ffn_b_fc, dtype=np.float32).reshape(1, HID))
    wproj = np.ascontiguousarray(np.asarray(L0_ffn_w_proj, dtype=np.float32))
    bproj = np.ascontiguousarray(np.asarray(L0_ffn_b_proj, dtype=np.float32).reshape(1, D))

    if "nc" not in _CACHE:
        _CACHE["nc"] = build()
    nc = _CACHE["nc"]

    in_maps = []
    for c in range(NCORES):
        rows = _slot_rows(c)
        x_qc = np.concatenate([x[a:b_] for (a, b_) in rows], axis=0)
        in_maps.append({
            "x_kv": np.ascontiguousarray(x[(c % 2) * S:(c % 2 + 1) * S]),
            "x_q": np.ascontiguousarray(x_qc),
            "w_qkv": wqkv, "w_fc": wfc, "b_fc": bfc,
            "w_proj": wproj, "b_proj": bproj,
            "masks": _masks_for_core(c),
        })
    res = run_bass_kernel_spmd(nc, in_maps, core_ids=list(range(NCORES)))
    out = np.empty((B * S, D), dtype=np.float32)
    for c in range(NCORES):
        yc = res.results[c]["y"]
        for si, (a, b_) in enumerate(_slot_rows(c)):
            out[a:b_] = yc[si * P:(si + 1) * P]
    return out.reshape(B, S, D)
